# version 10
# baseline (speedup 1.0000x reference)
import numpy as np
import ml_dtypes
import concourse.bass as bass
import concourse.mybir as mybir
from concourse.bass_utils import run_bass_kernel_spmd
from contextlib import ExitStack

F32 = mybir.dt.float32
BF16 = mybir.dt.bfloat16
U8 = mybir.dt.uint8
AF = mybir.ActivationFunctionType
ALU = mybir.AluOpType

T, C, H, D, NF = 2048, 1024, 16, 64, 4096
TCH, CCH, FCH = T // 128, C // 128, NF // 128
EPS = 1e-5
KB = 1024

ENGS = ("sp", "pool", "pe", "act", "dve")


class Sem:
    __slots__ = ("h", "n", "name")

    def __init__(self, h, name):
        self.h, self.n, self.name = h, 0, name


class B:

    def __init__(self, nc, stack):
        self.nc = nc
        self.stack = stack
        self.q = {e: [] for e in ENGS}
        self.waited = {e: {} for e in ENGS}

    def sem(self, name):
        return Sem(self.stack.enter_context(self.nc.semaphore(name)), name)

    def wait(self, eng, sem, val=None):
        v = sem.n if val is None else val
        assert v <= sem.n, f"wait on future value {v} > {sem.n} of {sem.name}"
        if v <= 0:
            return
        w = self.waited[eng]
        if w.get(sem.name, -1) >= v:
            return
        w[sem.name] = v
        self.q[eng].append(lambda e, s=sem, vv=v: e.wait_ge(s.h, vv))

    def op(self, eng, fn, inc=None, inc_n=1):
        if inc is not None:
            inc.n += inc_n

        def run(e, fn=fn, inc=inc, inc_n=inc_n):
            ins = fn(e)
            if inc is not None:
                ins.then_inc(inc.h, inc_n)

        self.q[eng].append(run)


def build_nc(stop=8, dbg=False):
    nc = bass.Bass("TRN2", target_bir_lowering=False)
    dt_ = nc.dram_tensor
    if dbg:
        dbg_fm = dt_("dbg_fm", [128, CCH * T], BF16, kind="ExternalOutput").ap()
        dbg_fm2 = dt_("dbg_fm2", [128, CCH * T], BF16, kind="ExternalOutput").ap()
        dbg_tm = dt_("dbg_tm", [T, C], F32, kind="ExternalOutput").ap()
        dbg_bf = dt_("dbg_bf", [128, TCH * 1040], BF16, kind="ExternalOutput").ap()
    x_h = dt_("x", [T, C], F32, kind="ExternalInput").ap()
    KT = {k: dt_(f"{k}T", [C, C], F32, kind="ExternalInput").ap()
          for k in ("qK", "kK", "vK", "pK")}
    V = {k: dt_(k, [C, C], F32, kind="ExternalInput").ap()
         for k in ("qV", "kV", "vV", "pV")}
    fKT_h = dt_("fKT", [C, NF], F32, kind="ExternalInput").ap()
    fV_h = dt_("fV", [NF, C], F32, kind="ExternalInput").ap()
    w1T_h = dt_("w1T", [C, NF], F32, kind="ExternalInput").ap()
    w2T_h = dt_("w2T", [C, NF], F32, kind="ExternalInput").ap()
    h1T_h = dt_("h1T", [NF, NF], F32, kind="ExternalInput").ap()
    h2T_h = dt_("h2T", [NF, NF], F32, kind="ExternalInput").ap()
    w3T_h = dt_("w3T", [NF, C], F32, kind="ExternalInput").ap()
    sc_h = {nm: dt_(nm, [128, FCH if nm != "s3" else CCH], F32,
                    kind="ExternalInput").ap()
            for nm in ("s1", "s2", "sh1", "sh2", "s3")}
    idb_h = dt_("idb", [128, 128], BF16, kind="ExternalInput").ap()
    idf_h = dt_("idf", [128, 128], F32, kind="ExternalInput").ap()
    mask_h = dt_("mask", [128, 896], BF16, kind="ExternalInput").ap()
    xn_park = dt_("xn_park", [T, C], F32, kind="Internal").ap()
    xnfm_park = dt_("xnfm_park", [128, CCH * T], BF16, kind="Internal").ap()
    a1_park = dt_("a1_park", [T, C], F32, kind="Internal").ap()
    acc_park = dt_("acc_park", [T, C], F32, kind="Internal").ap()
    out_h = dt_("out", [T, C], F32, kind="ExternalOutput").ap()

    st = ExitStack()
    with st:
        en = st.enter_context
        idb = en(nc.sbuf_tensor("idb_sb", [128, 128], BF16))
        idf = en(nc.sbuf_tensor("idf_sb", [128, 128], F32))
        masks = en(nc.sbuf_tensor("masks_sb", [128, 896], BF16))
        sc = {nm: en(nc.sbuf_tensor(nm + "_sb", [128, FCH if nm != "s3" else CCH], F32))
              for nm in ("s1", "s2", "sh1", "sh2", "s3")}
        _lnt = [en(nc.sbuf_tensor(f"lns{i}", [128, 20], F32)) for i in range(2)]
        LNS = [{"stats": t[:, 0:12], "mv": t[:, 12:14], "veps": t[:, 14:15],
                "sqv": t[:, 15:16], "rstd": t[:, 16:17], "nbias": t[:, 17:18]}
               for t in _lnt]
        rcp = en(nc.sbuf_tensor("rcp", [128, 4], F32))
        epsc = en(nc.sbuf_tensor("epsc", [128, 1], F32))
        xt = [en(nc.sbuf_tensor(f"xt{i}", [128, C], F32)) for i in range(3)]
        sbf = [en(nc.sbuf_tensor(f"sbf{i}", [128, C], BF16)) for i in range(2)]
        sf32 = en(nc.sbuf_tensor("sf32", [128, C], F32))
        wlog = en(nc.sbuf_tensor("wlog", [128, 4, CCH, 128], BF16))
        wbig = en(nc.sbuf_tensor("wbig", [128, 2, FCH, 128], BF16))

        R0, R1, R2, R3, R4 = 0, 33280, 66048, 98816, 131648
        AR_BYTES = 164416
        arena = en(nc.sbuf_tensor("arena", [128, AR_BYTES], U8))
        arb = arena.bitcast(BF16)
        arf = arena.bitcast(F32)

        def bfr(off_b, n):
            return arb[:, off_b // 2: off_b // 2 + n]

        def f32r(off_b, n):
            return arf[:, off_b // 4: off_b // 4 + n]

        h_fm = bfr(R0, CCH * T).rearrange("p (j t) -> p j t", t=T)
        vext = bfr(R0, TCH * 1040).rearrange("p (i e) -> p i e", e=1040)
        a_tf = bfr(R0, FCH * 1024).rearrange("p (j t) -> p j t", t=1024)
        u_buf = bfr(R0, FCH * 1024).rearrange("p (j t) -> p j t", t=1024)
        xn_fm = bfr(R1, CCH * T).rearrange("p (j t) -> p j t", t=T)
        vbufB = bfr(R1, CCH * 1025).rearrange("p (j e) -> p j e", e=1025)
        o_tm = bfr(R1, TCH * C).rearrange("p (i c) -> p i c", c=C)
        vbufD = bfr(R1, CCH * 1025).rearrange("p (j e) -> p j e", e=1025)
        a_t = bfr(R2, CCH * T).rearrange("p (j t) -> p j t", t=T)
        p_t = [bfr(R2 + 16384 * i, 16 * 512).rearrange("p (k t) -> p k t", t=512)
               for i in range(2)]
        fvbuf = bfr(R2, FCH * 1025).rearrange("p (j e) -> p j e", e=1025)
        g_buf = bfr(R2, FCH * 1024).rearrange("p (j t) -> p j t", t=1024)
        q_fm = bfr(R3, CCH * T).rearrange("p (j t) -> p j t", t=T)
        o_fm = bfr(R3, CCH * T).rearrange("p (j t) -> p j t", t=T)
        k_fm = bfr(R4, CCH * T).rearrange("p (j t) -> p j t", t=T)
        lnafm = bfr(R4, CCH * T).rearrange("p (j t) -> p j t", t=T)
        xn_ffn = bfr(R4, CCH * 1024).rearrange("p (j t) -> p j t", t=1024)
        y_buf = f32r(R2, CCH * 1024).rearrange("p (j t) -> p j t", t=1024)
        sf32b = xt[1]

        pq = [en(nc.psum_tensor(f"pq{i}", [128, 1024], F32)) for i in range(4)]
        pb = [pq[i // 2][:, (i % 2) * 512:(i % 2 + 1) * 512] for i in range(8)]
        pqb = [t.bitcast(BF16) for t in pq]
        pbb = [pqb[i // 2][:, (i % 2) * 1024:(i % 2 + 1) * 1024] for i in range(8)]

        b = B(nc, st)
        cPE = b.sem("cPE")
        cACT = b.sem("cACT")
        cDVE = b.sem("cDVE")
        cPOOL = b.sem("cPOOL")
        dSP = b.sem("dSP")
        dPL = b.sem("dPL")
        sOUT = b.sem("sOUT")

        ln_last_act = [0, 0]

        def ln_tile(src, dst, st=0, src_dve=None, src_act=None, dst_pe=None):
            S = LNS[st]
            b.wait("dve", cACT, ln_last_act[st] if src_act is None else
                   max(ln_last_act[st], src_act))
            if src_act is None and src_dve is None:
                b.wait("dve", cDVE)
            elif src_dve is not None:
                b.wait("dve", cDVE, src_dve)
            b.op("dve", lambda e: e.bn_stats(S["stats"][:, 0:6], src[:, 0:512]))
            b.op("dve", lambda e: e.bn_stats(S["stats"][:, 6:12], src[:, 512:1024]),
                 inc=cDVE)
            b.wait("dve", cDVE)
            b.op("dve", lambda e: e.bn_aggr(S["mv"][:], S["stats"][:]), inc=cDVE)
            b.wait("act", cDVE)
            b.op("act", lambda e: e.activation(out=S["sqv"][:], in_=S["mv"][:, 1:2],
                                               func=AF.Ln, bias=epsc[:]), inc=cACT)
            b.wait("act", cACT)
            b.op("act", lambda e: e.activation(out=S["rstd"][:], in_=S["sqv"][:],
                                               func=AF.Exp, scale=-0.5), inc=cACT)
            b.wait("dve", cACT)
            b.op("dve", lambda e: e.scalar_tensor_tensor(
                S["nbias"][:], S["mv"][:, 0:1], -1.0, S["rstd"][:], ALU.mult,
                ALU.mult), inc=cDVE)
            b.wait("act", cDVE)
            if dst_pe is None:
                b.wait("act", cPE)
            else:
                b.wait("act", cPE, dst_pe)
            b.op("act", lambda e: e.activation(
                out=dst, in_=src, func=AF.Identity, scale=S["rstd"][:],
                bias=S["nbias"][:]), inc=cACT)
            ln_last_act[st] = cACT.n

        def transpose_tm_to_fm(src_bf, dst_fm, tc, bank):
            b.wait("pe", cDVE)
            b.wait("pe", cACT)
            pt = pbb[bank]
            for j in range(CCH):
                b.op("pe", lambda e, j=j: e.transpose(
                    pt[:, j * 128:(j + 1) * 128],
                    src_bf[:, j * 128:(j + 1) * 128], idb[:]),
                    inc=cPE if j == CCH - 1 else None)
            b.wait("dve", cPE)
            src_v = pt[:, 0:1024].rearrange("p (j t) -> p j t", t=128)
            b.op("dve", lambda e: e.tensor_copy(
                dst_fm[:, :, tc * 128:(tc + 1) * 128], src_v), inc=cDVE)

        def logits_stage(KT_ap, rhs_fm, nch, dst_at, tt_n, tt_off, scale, banks0,
                         banks1):
            b.wait("pe", cDVE)
            mm_done, exp_done = {}, {}
            for n in range(nch):
                par = n % 2
                bks = banks0 if par == 0 else banks1
                if n >= 3:
                    b.wait("pool", cPE, mm_done[n - 3])
                else:
                    b.wait("pool", cPE)
                b.op("pool", lambda e, n=n: e.dma_start(
                    out=wlog[:, n % 4, :, :],
                    in_=KT_ap[:, n * 128:(n + 1) * 128].rearrange(
                        "(j p) n -> p j n", p=128)), inc=dPL, inc_n=16)
                b.wait("pe", dPL)
                if n >= 2:
                    b.wait("pe", cACT, exp_done[n - 2])
                for c in range(CCH):
                    for tt in range(tt_n):
                        b.op("pe", lambda e, n=n, c=c, tt=tt, bks=bks: e.matmul(
                            pb[bks[tt]][:], wlog[:, n % 4, c, :],
                            rhs_fm[:, c, tt_off + tt * 512: tt_off + (tt + 1) * 512],
                            start=(c == 0), stop=(c == CCH - 1)),
                            inc=cPE if (c == CCH - 1 and tt == tt_n - 1) else None)
                mm_done[n] = cPE.n
                b.wait("act", cPE, mm_done[n])
                for tp in range(tt_n // 2):
                    qi = bks[2 * tp] // 2
                    b.op("act", lambda e, n=n, tp=tp, qi=qi: e.activation(
                        out=dst_at[:, n, tt_off + tp * 1024: tt_off + (tp + 1) * 1024],
                        in_=pq[qi][:], func=AF.Exp, scale=scale), inc=cACT)
                exp_done[n] = cACT.n

        def ov_stage(vbuf, src_at, nch, dst_kind, dst, tc_range, t_base, c0b, c1b,
                     ssb, extra=None):
            b.wait("pe", cDVE)
            evac_done, stg_ready, tr_pe, tr_ev = {}, {}, {}, {}
            ex_ready = {}
            pend = None

            def emit_transpose(i, tcv):
                stg = sbf[i % 2]
                b.wait("pe", cDVE, stg_ready[i])
                if i - 1 in tr_ev:
                    b.wait("pe", cDVE, tr_ev[i - 1])
                pt = pbb[7]
                for j in range(CCH):
                    b.op("pe", lambda e, j=j, pt=pt, stg=stg: e.transpose(
                        pt[:, j * 128:(j + 1) * 128],
                        stg[:, j * 128:(j + 1) * 128], idb[:]),
                        inc=cPE if j == CCH - 1 else None)
                tr_pe[i] = cPE.n
                b.wait("dve", cPE, tr_pe[i])
                src_v = pt[:, 0:1024].rearrange("p (j t) -> p j t", t=128)
                b.op("dve", lambda e, tcv=tcv, src_v=src_v: e.tensor_copy(
                    dst[:, :, tcv * 128:(tcv + 1) * 128], src_v), inc=cDVE)
                tr_ev[i] = cDVE.n

            for idx, tc in enumerate(tc_range):
                par = idx % 2
                bc0, bc1, bs = c0b[par], c1b[par], ssb[par]
                if idx >= 2:
                    sem2, val2 = evac_done[idx - 2]
                    b.wait("pe", sem2, val2)
                lt = (tc - t_base) * 128
                for n in range(nch):
                    lhs = src_at[:, n, lt:lt + 128]
                    b.op("pe", lambda e, lhs=lhs, n=n, bc0=bc0: e.matmul(
                        pb[bc0][:], lhs, vbuf[:, n, 0:512],
                        start=(n == 0), stop=(n == nch - 1)))
                    b.op("pe", lambda e, lhs=lhs, n=n, bc1=bc1: e.matmul(
                        pb[bc1][:], lhs, vbuf[:, n, 512:1024],
                        start=(n == 0), stop=(n == nch - 1)))
                    b.op("pe", lambda e, lhs=lhs, n=n, bs=bs: e.matmul(
                        pb[bs][:, 0:1], lhs, vbuf[:, n, 1024:1025],
                        start=(n == 0), stop=(n == nch - 1)),
                        inc=cPE if n == nch - 1 else None)
                mm = cPE.n
                if pend is not None and dst_kind == "qk":
                    emit_transpose(*pend)
                    pend = None
                b.wait("dve", cPE, mm)
                if idx >= 2 and dst_kind == "f32":
                    sem2, val2 = evac_done[idx - 2]
                    b.wait("dve", sem2, val2)
                b.op("dve", lambda e, bs=bs, par=par: e.reciprocal(
                    rcp[:, par:par + 1], pb[bs][:, 0:1]), inc=cDVE)
                rcp_done = cDVE.n
                if dst_kind == "qk":
                    stg = sbf[par]
                    b.wait("dve", cDVE, rcp_done)
                    b.op("dve", lambda e, par=par, stg=stg: e.tensor_scalar(
                        stg[:, 0:1024], pq[par][:], rcp[:, par:par + 1], None,
                        ALU.mult), inc=cDVE)
                    evac_done[idx] = (cDVE, cDVE.n)
                    stg_ready[idx] = cDVE.n
                    pend = (idx, tc)
                elif dst_kind == "vext":
                    vv = vext[:, tc, 0:1040].rearrange("p (h e) -> p h e", e=65)
                    b.wait("dve", cDVE, rcp_done)
                    b.op("dve", lambda e, par=par, vv=vv: e.tensor_scalar(
                        vv[:, :, 0:64],
                        pq[par][:].rearrange("p (h d) -> p h d", d=64),
                        rcp[:, par:par + 1], None, ALU.mult), inc=cDVE)
                    evac_done[idx] = (cDVE, cDVE.n)
                else:
                    sg = sf32 if par == 0 else sf32b
                    b.wait("act", cDVE, rcp_done)
                    b.wait("act", cPE, mm)
                    b.op("act", lambda e, par=par, sg=sg: e.activation(
                        out=sg[:, 0:1024], in_=pq[par][:], func=AF.Identity,
                        scale=rcp[:, par:par + 1]), inc=cACT)
                    evac_done[idx] = (cACT, cACT.n)
                    if pend is not None:
                        extra(pend[1], sf32 if pend[0] % 2 == 0 else sf32b,
                              ex_ready[pend[0]])
                    ex_ready[idx] = cACT.n
                    pend = (idx, tc)
            if pend is not None:
                if dst_kind == "qk":
                    emit_transpose(*pend)
                else:
                    extra(pend[1], sf32 if pend[0] % 2 == 0 else sf32b,
                          ex_ready[pend[0]])

        def transpose_tm_to_fm_from_dve(src_bf, dst_fm, tc, bank=7):
            b.wait("pe", cDVE)
            pt = pbb[bank]
            for j in range(CCH):
                b.op("pe", lambda e, j=j: e.transpose(
                    pt[:, j * 128:(j + 1) * 128],
                    src_bf[:, j * 128:(j + 1) * 128], idb[:]),
                    inc=cPE if j == CCH - 1 else None)
            b.wait("dve", cPE)
            src_v = pt[:, 0:1024].rearrange("p (j t) -> p j t", t=128)
            b.op("dve", lambda e: e.tensor_copy(
                dst_fm[:, :, tc * 128:(tc + 1) * 128], src_v), inc=cDVE)

        b.op("sp", lambda e: e.dma_start(out=idb[:], in_=idb_h), inc=dSP, inc_n=16)
        b.op("sp", lambda e: e.dma_start(out=idf[:], in_=idf_h), inc=dSP, inc_n=16)
        b.op("sp", lambda e: e.dma_start(out=masks[:], in_=mask_h),
             inc=dSP, inc_n=16)
        for nm in ("s1", "s2", "sh1", "sh2", "s3"):
            b.op("sp", lambda e, nm=nm: e.dma_start(out=sc[nm][:], in_=sc_h[nm]),
                 inc=dSP, inc_n=16)
        consts_loaded = dSP.n
        b.op("pool", lambda e: e.memset(epsc[:], EPS), inc=cPOOL)
        b.wait("act", cPOOL)
        b.wait("dve", cPOOL)

        b.op("pool", lambda e: e.dma_start(
            out=qkw[:, :, :],
            in_=KT["qK"].rearrange("(j p) n -> p j n", p=128)), inc=dPL, inc_n=16)
        qkw_d = dPL.n
        xn_stage = [sf32, f32r(R3, C)]
        qkw = bfr(R3 + 4096, CCH * 1024).rearrange("p (j n) -> p j n", n=1024)
        ab_mm, ab_ex = {}, {}
        a_done = {}
        park_done = {}
        a_info = {}
        for tc in range(TCH):
            slot = xt[tc % 2]
            if tc >= 2:
                ai = a_info[tc - 2]
                b.wait("sp", cDVE, ai["ln1_stats"])
                b.wait("sp", cACT, ai["ln1_norm"])
            b.op("sp", lambda e, tc=tc, s=slot: e.dma_start(
                out=s[:], in_=x_h[tc * 128:(tc + 1) * 128, :]), inc=dSP, inc_n=16)
            ld = dSP.n
            xs = xn_stage[tc % 2]
            if tc >= 2:
                ai = a_info[tc - 2]
                b.wait("act", dSP, park_done[tc - 2])
                b.wait("act", cDVE, max(ai["ln2_stats"], ai["bf_copy"]))
            b.wait("dve", dSP, ld)
            ln_tile(slot[:], xs[:, 0:C], st=tc % 2)
            ln1_norm = cACT.n
            ln1_stats = cDVE.n
            b.wait("sp", cACT, ln1_norm)
            b.op("sp", lambda e, tc=tc, xs=xs: e.dma_start(
                out=xn_park[tc * 128:(tc + 1) * 128, :], in_=xs[:, 0:C]),
                inc=dSP, inc_n=16)
            park_done[tc] = dSP.n
            b.wait("dve", cACT, ln1_norm)
            if tc >= 1:
                b.wait("dve", cPE, a_info[tc - 1]["tr_xn"])
            b.op("dve", lambda e, xs=xs: e.tensor_copy(sbf[0][:], xs[:, 0:C]),
                 inc=cDVE)
            bf_copy = cDVE.n
            if tc >= 1:
                b.wait("act", cPE, a_info[tc - 1]["tr_h"])
            ln_tile(xs[:, 0:C], sbf[1][:], st=tc % 2, src_act=ln1_norm)
            ln2_norm = cACT.n
            ln2_stats = cDVE.n
            b.wait("pe", cDVE, bf_copy)
            if tc >= 1:
                b.wait("pe", cDVE, a_info[tc - 1]["ev_xn"])
            for j in range(CCH):
                b.op("pe", lambda e, j=j: e.transpose(
                    pbb[6][:, j * 128:(j + 1) * 128],
                    sbf[0][:, j * 128:(j + 1) * 128], idb[:]),
                    inc=cPE if j == CCH - 1 else None)
            tr_xn = cPE.n
            b.wait("dve", cPE, tr_xn)
            b.op("dve", lambda e, tc=tc: e.tensor_copy(
                xn_fm[:, :, tc * 128:(tc + 1) * 128],
                pbb[6][:, 0:1024].rearrange("p (j t) -> p j t", t=128)), inc=cDVE)
            ev_xn = cDVE.n
            b.wait("pe", cACT, ln2_norm)
            if tc >= 1:
                b.wait("pe", cDVE, a_info[tc - 1]["ev_h"])
            for j in range(CCH):
                b.op("pe", lambda e, j=j: e.transpose(
                    pbb[7][:, j * 128:(j + 1) * 128],
                    sbf[1][:, j * 128:(j + 1) * 128], idb[:]),
                    inc=cPE if j == CCH - 1 else None)
            tr_h = cPE.n
            b.wait("dve", cPE, tr_h)
            b.op("dve", lambda e, tc=tc: e.tensor_copy(
                h_fm[:, :, tc * 128:(tc + 1) * 128],
                pbb[7][:, 0:1024].rearrange("p (j t) -> p j t", t=128)), inc=cDVE)
            ev_h = cDVE.n
            a_info[tc] = dict(ln1_stats=ln1_stats, ln1_norm=ln1_norm,
                              ln2_stats=ln2_stats, bf_copy=bf_copy,
                              tr_xn=tr_xn, tr_h=tr_h, ev_xn=ev_xn, ev_h=ev_h)
            if tc % 4 == 3:
                tt = tc // 4
                b.wait("pe", dPL, qkw_d)
                b.wait("pe", cDVE, ev_h)
                for n in range(CCH):
                    gi = tt * CCH + n
                    if gi >= 4:
                        b.wait("pe", cACT, ab_ex[gi - 4])
                    for c in range(CCH):
                        b.op("pe", lambda e, gi=gi, n=n, c=c, tt=tt: e.matmul(
                            pb[gi % 4][:], qkw[:, c, n * 128:(n + 1) * 128],
                            h_fm[:, c, tt * 512:(tt + 1) * 512],
                            start=(c == 0), stop=(c == CCH - 1)),
                            inc=cPE if c == CCH - 1 else None)
                    ab_mm[gi] = cPE.n
                    if gi >= 2:
                        b.wait("act", cPE, ab_mm[gi - 2])
                        b.op("act", lambda e, gj=gi - 2: e.activation(
                            out=a_t[:, gj % CCH, (gj // CCH) * 512:
                                    (gj // CCH + 1) * 512],
                            in_=pb[gj % 4][:], func=AF.Exp, scale=1 / 32.0),
                            inc=cACT)
                        ab_ex[gi - 2] = cACT.n
                if tt == 3:
                    for gj in (30, 31):
                        b.wait("act", cPE, ab_mm[gj])
                        b.op("act", lambda e, gj=gj: e.activation(
                            out=a_t[:, gj % CCH, (gj // CCH) * 512:
                                    (gj // CCH + 1) * 512],
                            in_=pb[gj % 4][:], func=AF.Exp, scale=1 / 32.0),
                            inc=cACT)
                        ab_ex[gj] = cACT.n
        b.wait("sp", cDVE)
        b.op("sp", lambda e: e.dma_start(
            out=xnfm_park, in_=xn_fm.rearrange("p j t -> p (j t)")),
            inc=dSP, inc_n=16)
        xnfm_parked = dSP.n

        if stop <= 1:
            if dbg:
                b.wait("sp", cDVE)
                b.op("sp", lambda e: e.dma_start(
                    out=dbg_fm, in_=h_fm.rearrange("p j t -> p (j t)")),
                    inc=sOUT, inc_n=16)
                b.op("sp", lambda e: e.dma_start(
                    out=dbg_fm2, in_=xn_fm.rearrange("p j t -> p (j t)")),
                    inc=sOUT, inc_n=16)
            b.wait("sp", dSP)
            b.wait("sp", sOUT)
            return finish(nc, b)
        OVB = ((0, 2), (1, 3), (4, 5))

        b.wait("pool", dSP, xnfm_parked)
        b.op("pool", lambda e: e.dma_start(
            out=vbufB[:, :, 0:1024],
            in_=V["qV"].rearrange("(j p) c -> p j c", p=128)), inc=dPL, inc_n=16)
        b.op("pool", lambda e: e.memset(vbufB[:, :, 1024:1025], 1.0), inc=cPOOL)
        b.wait("pe", dPL)
        b.wait("pe", cPOOL)
        b.wait("pe", cACT)
        ov_stage(vbufB, a_t, CCH, "qk", q_fm, range(TCH), 0, *OVB)

        if stop <= 2:
            if dbg:
                b.wait("sp", cDVE)
                b.op("sp", lambda e: e.dma_start(
                    out=dbg_fm, in_=q_fm.rearrange("p j t -> p (j t)")),
                    inc=sOUT, inc_n=16)
                b.op("sp", lambda e: e.dma_start(
                    out=dbg_fm2, in_=a_t.rearrange("p j t -> p (j t)")),
                    inc=sOUT, inc_n=16)
            b.wait("sp", dSP)
            b.wait("sp", sOUT)
            return finish(nc, b)
        logits_stage(KT["kK"], h_fm, CCH, a_t, 4, 0, 1 / 32.0,
                     [0, 1, 2, 3], [4, 5, 6, 7])
        b.wait("pool", cPE)
        b.op("pool", lambda e: e.dma_start(
            out=vbufB[:, :, 0:1024],
            in_=V["kV"].rearrange("(j p) c -> p j c", p=128)), inc=dPL, inc_n=16)
        b.wait("pe", dPL)
        b.wait("pe", cACT)
        ov_stage(vbufB, a_t, CCH, "qk", k_fm, range(TCH), 0, *OVB)

        if stop <= 3:
            if dbg:
                b.wait("sp", cDVE)
                b.op("sp", lambda e: e.dma_start(
                    out=dbg_fm, in_=k_fm.rearrange("p j t -> p (j t)")),
                    inc=sOUT, inc_n=16)
            b.wait("sp", dSP)
            b.wait("sp", sOUT)
            return finish(nc, b)
        logits_stage(KT["vK"], h_fm, CCH, a_t, 4, 0, 1 / 32.0,
                     [0, 1, 2, 3], [4, 5, 6, 7])
        b.wait("pool", cPE)
        b.op("pool", lambda e: e.dma_start(
            out=vbufB[:, :, 0:1024],
            in_=V["vV"].rearrange("(j p) c -> p j c", p=128)), inc=dPL, inc_n=16)
        b.op("pool", lambda e: e.memset(
            vext[:, :, 0:1040].rearrange("p i (h e) -> p i h e", e=65)[:, :, :, 64:65],
            1.0), inc=cPOOL)
        b.wait("pe", dPL)
        b.wait("pe", cPOOL)
        b.wait("pe", cACT)
        ov_stage(vbufB, a_t, CCH, "vext", None, range(TCH), 0, *OVB)

        if stop <= 4:
            if dbg:
                b.wait("sp", cDVE)
                b.op("sp", lambda e: e.dma_start(
                    out=dbg_bf, in_=vext.rearrange("p i e -> p (i e)")),
                    inc=sOUT, inc_n=16)
            b.wait("sp", dSP)
            b.wait("sp", sOUT)
            return finish(nc, b)
        p_t2 = bfr(R2, 16 * 1024).rearrange("p (k t) -> p k t", t=1024)
        b.wait("pe", dSP, consts_loaded)
        b.wait("pe", cDVE)
        exp_par = {0: 0, 1: 0}
        o_evac = {}
        gidx = 0
        for hp in range(CCH):
            hA, hB = 2 * hp, 2 * hp + 1
            for qt in range(4):
                kmax = 4 * qt + 4
                mask_snap = cDVE.n
                for kt in range(kmax):
                    par = kt % 2
                    bA, bB = 2 * par, 2 * par + 1
                    b.wait("pe", cACT, exp_par[par])
                    b.op("pe", lambda e, hp=hp, kt=kt, qt=qt, bA=bA: e.matmul(
                        pb[bA][:],
                        k_fm[0:64, hp, kt * 128:(kt + 1) * 128],
                        q_fm[0:64, hp, qt * 512:(qt + 1) * 512],
                        start=True, stop=True, tile_position=(0, 0)))
                    b.op("pe", lambda e, hp=hp, kt=kt, qt=qt, bB=bB: e.matmul(
                        pb[bB][:],
                        k_fm[64:128, hp, kt * 128:(kt + 1) * 128],
                        q_fm[64:128, hp, qt * 512:(qt + 1) * 512],
                        start=True, stop=True, tile_position=(64, 0)), inc=cPE)
                    b.wait("act", cPE, cPE.n)
                    b.op("act", lambda e, kt=kt, par=par: e.activation(
                        out=p_t2[:, kt, :], in_=pq[par][:], func=AF.Exp,
                        scale=0.125), inc=cACT)
                    exp_par[par] = cACT.n
                    if kt >= 4 * qt:
                        r = kt - 4 * qt
                        off = 384 - 128 * r
                        b.wait("dve", cACT, cACT.n)
                        b.op("dve", lambda e, kt=kt, off=off: e.tensor_tensor(
                            p_t2[:, kt, 0:512], p_t2[:, kt, 0:512],
                            masks[:, off:off + 512], ALU.mult))
                        b.op("dve", lambda e, kt=kt, off=off: e.tensor_tensor(
                            p_t2[:, kt, 512:1024], p_t2[:, kt, 512:1024],
                            masks[:, off:off + 512], ALU.mult), inc=cDVE)
                        mask_snap = cDVE.n
                exp_all = max(exp_par.values())
                b.wait("pe", cACT, exp_all)
                b.wait("pe", cDVE, mask_snap)
                for h2 in range(2):
                    hh = hA if h2 == 0 else hB
                    for sub in range(4):
                        bank = 4 + gidx % 4
                        if gidx - 4 in o_evac:
                            b.wait("pe", cDVE, o_evac[gidx - 4])
                        for kt in range(kmax):
                            b.op("pe", lambda e, h2=h2, kt=kt, sub=sub, hh=hh,
                                 bank=bank, kmax=kmax: e.matmul(
                                     pb[bank][:, 0:65],
                                     p_t2[:, kt, h2 * 512 + sub * 128:
                                          h2 * 512 + (sub + 1) * 128],
                                     vext[:, kt, hh * 65:hh * 65 + 65],
                                     start=(kt == 0), stop=(kt == kmax - 1)),
                                 inc=cPE if kt == kmax - 1 else None)
                        b.wait("dve", cPE, cPE.n)
                        rs = 2 + gidx % 2
                        b.op("dve", lambda e, bank=bank, rs=rs: e.reciprocal(
                            rcp[:, rs:rs + 1], pb[bank][:, 64:65]), inc=cDVE)
                        b.wait("dve", cDVE)
                        b.op("dve", lambda e, bank=bank, qt=qt, sub=sub, hh=hh,
                             rs=rs: e.tensor_scalar(
                                 o_tm[:, qt * 4 + sub, hh * 64:(hh + 1) * 64],
                                 pb[bank][:, 0:64], rcp[:, rs:rs + 1], None,
                                 ALU.mult), inc=cDVE)
                        o_evac[gidx] = cDVE.n
                        gidx += 1

        if stop <= 5:
            if dbg:
                b.wait("sp", cDVE)
                b.op("sp", lambda e: e.dma_start(
                    out=dbg_bf[:, 0:TCH * 1024],
                    in_=o_tm.rearrange("p i c -> p (i c)")), inc=sOUT, inc_n=16)
            b.wait("sp", dSP)
            b.wait("sp", sOUT)
            return finish(nc, b)
        for tc in range(TCH):
            transpose_tm_to_fm_from_dve(o_tm[:, tc, :], o_fm, tc, bank=7)
        logits_stage(KT["pK"], o_fm, CCH, a_t, 4, 0, 1 / 32.0,
                     [0, 1, 2, 3], [4, 5, 6, 7])
        b.wait("pool", cPE)
        b.op("pool", lambda e: e.dma_start(
            out=vbufD[:, :, 0:1024],
            in_=V["pV"].rearrange("(j p) c -> p j c", p=128)), inc=dPL, inc_n=16)
        b.op("pool", lambda e: e.memset(vbufD[:, :, 1024:1025], 1.0), inc=cPOOL)
        b.wait("pe", dPL)
        b.wait("pe", cPOOL)
        b.wait("pe", cACT)

        d_park = {}
        d_lnr = {}
        d_pend = {}
        d_trev = {}

        def d_tr_flush(tc):
            snap = d_pend.pop(tc)
            stg = sbf[tc % 2]
            b.wait("pe", cACT, snap)
            if tc - 1 in d_trev:
                b.wait("pe", cDVE, d_trev[tc - 1])
            pt = pbb[7]
            for j in range(CCH):
                b.op("pe", lambda e, j=j, pt=pt, stg=stg: e.transpose(
                    pt[:, j * 128:(j + 1) * 128],
                    stg[:, j * 128:(j + 1) * 128], idb[:]),
                    inc=cPE if j == CCH - 1 else None)
            b.wait("dve", cPE, cPE.n)
            src_v = pt[:, 0:1024].rearrange("p (j t) -> p j t", t=128)
            b.op("dve", lambda e, tc=tc, src_v=src_v: e.tensor_copy(
                lnafm[:, :, tc * 128:(tc + 1) * 128], src_v), inc=cDVE)
            d_trev[tc] = cDVE.n

        def d_extra(tc, sg, sg_act):
            slot = xt[2] if tc % 2 == 0 else xt[0]
            if tc >= 2:
                b.wait("sp", dSP, d_park[tc - 2])
                b.wait("sp", cACT, d_lnr[tc - 2])
            b.wait("sp", cPOOL)
            b.wait("sp", dSP, park_done[tc])
            b.op("sp", lambda e, tc=tc, s=slot: e.dma_start(
                out=s[:], in_=xn_park[tc * 128:(tc + 1) * 128, :]),
                inc=dSP, inc_n=16)
            ld = dSP.n
            b.wait("pool", dSP, ld)
            b.wait("pool", cACT, sg_act)
            b.op("pool", lambda e, s=slot, sg=sg: e.tensor_tensor(
                s[:], s[:], sg[:], ALU.add), inc=cPOOL)
            add_snap = cPOOL.n
            b.wait("sp", cPOOL, add_snap)
            b.op("sp", lambda e, tc=tc, s=slot: e.dma_start(
                out=a1_park[tc * 128:(tc + 1) * 128, :], in_=s[:]),
                inc=dSP, inc_n=16)
            d_park[tc] = dSP.n
            b.wait("dve", cPOOL, add_snap)
            ln_tile(slot[:], sbf[tc % 2][:], st=tc % 2)
            d_lnr[tc] = cACT.n
            d_pend[tc] = cACT.n
            if tc - 1 in d_pend:
                d_tr_flush(tc - 1)

        ov_stage(vbufD, a_t, CCH, "f32", None, range(TCH), 0, *OVB, extra=d_extra)
        for _tc in sorted(d_pend):
            d_tr_flush(_tc)

        if stop <= 6:
            if dbg:
                b.wait("sp", cDVE)
                b.op("sp", lambda e: e.dma_start(
                    out=dbg_fm, in_=lnafm.rearrange("p j t -> p (j t)")),
                    inc=sOUT, inc_n=16)
                b.op("sp", lambda e: e.dma_start(
                    out=dbg_fm2, in_=o_fm.rearrange("p j t -> p (j t)")),
                    inc=sOUT, inc_n=16)
                b.wait("sp", dSP, d_park[TCH - 1])
                b.op("sp", lambda e: e.dma_start(out=dbg_tm, in_=a1_park),
                    inc=sOUT, inc_n=16)
            b.wait("sp", dSP)
            b.wait("sp", sOUT)
            return finish(nc, b)
        d_pe_done = cPE.n
        e_park = {}
        fv_d = None
        for pss in range(2):
            t_off = pss * 1024
            b.wait("pe", cDVE)
            e1_mm, e1_ex = {}, {}
            for n in range(FCH):
                par = n % 2
                bks = (0, 1) if par == 0 else (2, 3)
                if n >= 3:
                    b.wait("pool", cPE, e1_mm[n - 3])
                else:
                    b.wait("pool", cPE)
                b.op("pool", lambda e, n=n: e.dma_start(
                    out=wlog[:, n % 4, :, :],
                    in_=fKT_h[:, n * 128:(n + 1) * 128].rearrange(
                        "(j p) n -> p j n", p=128)), inc=dPL, inc_n=16)
                if pss == 0 and n < 8:
                    b.op("pool", lambda e, n=n: e.dma_start(
                        out=fvbuf[:, 4 * n:4 * n + 4, 0:1024],
                        in_=fV_h[n * 512:(n + 1) * 512, :].rearrange(
                            "(j p) c -> p j c", p=128)), inc=dPL, inc_n=16)
                    if n == 7:
                        b.op("pool", lambda e: e.memset(
                            fvbuf[:, :, 1024:1025], 1.0), inc=cPOOL)
                        fv_d, fv_c = dPL.n, cPOOL.n
                b.wait("pe", dPL)
                if n >= 2:
                    b.wait("pe", cACT, e1_ex[n - 2])
                for c in range(CCH):
                    for tt in range(2):
                        b.op("pe", lambda e, n=n, c=c, tt=tt, bks=bks, t_off=t_off:
                             e.matmul(
                                 pb[bks[tt]][:], wlog[:, n % 4, c, :],
                                 lnafm[:, c, t_off + tt * 512:t_off + (tt + 1) * 512],
                                 start=(c == 0), stop=(c == CCH - 1)),
                             inc=cPE if (c == CCH - 1 and tt == 1) else None)
                e1_mm[n] = cPE.n
                b.wait("act", cPE, e1_mm[n])
                b.op("act", lambda e, n=n, bks=bks: e.activation(
                    out=a_tf[:, n, 0:1024],
                    in_=pq[bks[0] // 2][:], func=AF.Exp, scale=1 / 32.0), inc=cACT)
                e1_ex[n] = cACT.n
            b.wait("pe", dPL, fv_d)
            b.wait("pe", cPOOL, fv_c)
            b.wait("pe", cACT)

            def e_extra(tc, sg, sg_act):
                slot = xt[2]
                if tc - 1 in e_park:
                    b.wait("sp", dSP, e_park[tc - 1])
                b.wait("sp", cPOOL)
                b.wait("sp", dSP, d_park[tc])
                b.op("sp", lambda e, tc=tc, s=slot: e.dma_start(
                    out=s[:], in_=a1_park[tc * 128:(tc + 1) * 128, :]),
                    inc=dSP, inc_n=16)
                ld = dSP.n
                b.wait("pool", dSP, ld)
                b.wait("pool", cACT, sg_act)
                b.op("pool", lambda e, s=slot, sg=sg: e.tensor_tensor(
                    s[:], s[:], sg[:], ALU.add), inc=cPOOL)
                b.wait("sp", cPOOL)
                b.op("sp", lambda e, tc=tc, s=slot: e.dma_start(
                    out=acc_park[tc * 128:(tc + 1) * 128, :], in_=s[:]),
                    inc=dSP, inc_n=16)
                e_park[tc] = dSP.n

            ov_stage(fvbuf, a_tf, FCH, "f32", None,
                     range(pss * 8, pss * 8 + 8), pss * 8, *OVB, extra=e_extra)
            e1_last = e1_mm

        if stop <= 7:
            if dbg:
                b.wait("sp", dSP, e_park[TCH - 1])
                b.op("sp", lambda e: e.dma_start(out=dbg_tm, in_=acc_park),
                    inc=sOUT, inc_n=16)
            b.wait("sp", dSP)
            b.wait("sp", sOUT)
            return finish(nc, b)
        out_done = {}
        f5_add = {}
        wbig_gate = None
        prev_f1 = None
        prev_f4ev = None
        f5_dve = 0
        for pss in range(2):
            t_off = pss * 1024
            if pss == 0:
                b.wait("sp", cPE, e1_last[FCH - 1])
            else:
                b.wait("sp", cPE, prev_f1[FCH - 1])
            b.wait("sp", dSP, xnfm_parked)
            b.op("sp", lambda e, t_off=t_off: e.dma_start(
                out=xn_ffn[:, :, :],
                in_=xnfm_park.rearrange("p (j t) -> p j t", t=T)[
                    :, :, t_off:t_off + 1024]),
                inc=dSP, inc_n=16)
            xn_ldd = dSP.n

            f1_mm, f1_ev = {}, {}
            for o in range(FCH):
                par = o % 2
                bks = (0, 1, 2, 3) if par == 0 else (4, 5, 6, 7)
                if o >= 3:
                    b.wait("pool", cPE, f1_mm[o - 3])
                elif pss == 0:
                    b.wait("pool", cPE, e1_last[FCH - 3 + o])
                else:
                    b.wait("pool", cPE, prev_f1[FCH - 3 + o])
                b.op("pool", lambda e, o=o: e.dma_start(
                    out=wlog[:, o % 4, :, :],
                    in_=w1T_h[:, o * 128:(o + 1) * 128].rearrange(
                        "(j p) n -> p j n", p=128)), inc=dPL, inc_n=16)
                if o >= 2:
                    b.wait("pool", cPE, f1_mm[o - 2])
                elif wbig_gate is not None:
                    b.wait("pool", cPE, wbig_gate[6 + o])
                b.op("pool", lambda e, o=o: e.dma_start(
                    out=wbig[:, o % 2, 0:CCH, :],
                    in_=w2T_h[:, o * 128:(o + 1) * 128].rearrange(
                        "(j p) n -> p j n", p=128)), inc=dPL, inc_n=16)
                b.wait("pe", dPL)
                b.wait("pe", dSP, xn_ldd)
                if o >= 2:
                    b.wait("pe", cDVE, f1_ev[o - 2])
                elif pss == 1:
                    if o == 0:
                        b.wait("pe", cACT, prev_f4ev)
                    else:
                        b.wait("pe", cDVE, f5_dve)
                else:
                    b.wait("pe", cDVE)
                for c in range(CCH):
                    for tt in range(2):
                        b.op("pe", lambda e, o=o, c=c, tt=tt, bks=bks: e.matmul(
                            pb[bks[tt]][:], wlog[:, o % 4, c, :],
                            xn_ffn[:, c, tt * 512:(tt + 1) * 512],
                            start=(c == 0), stop=(c == CCH - 1)))
                for c in range(CCH):
                    for tt in range(2):
                        b.op("pe", lambda e, o=o, c=c, tt=tt, bks=bks: e.matmul(
                            pb[bks[2 + tt]][:], wbig[:, o % 2, c, :],
                            xn_ffn[:, c, tt * 512:(tt + 1) * 512],
                            start=(c == 0), stop=(c == CCH - 1)),
                            inc=cPE if (c == CCH - 1 and tt == 1) else None)
                f1_mm[o] = cPE.n
                b.wait("act", cPE, f1_mm[o])
                b.wait("act", cDVE)
                b.op("act", lambda e, o=o, bks=bks: e.activation(
                    out=u_buf[:, o, 0:1024], in_=pq[bks[0] // 2][:], func=AF.Silu,
                    scale=sc["s1"][:, o:o + 1]), inc=cACT)
                b.op("act", lambda e, o=o, bks=bks: e.activation(
                    out=sbf[0][:, 0:1024], in_=pq[bks[2] // 2][:], func=AF.Identity,
                    scale=sc["s2"][:, o:o + 1]), inc=cACT)
                b.wait("dve", cACT)
                b.op("dve", lambda e, o=o: e.tensor_tensor(
                    u_buf[:, o, :], u_buf[:, o, :], sbf[0][:], ALU.mult), inc=cDVE)
                f1_ev[o] = cDVE.n

            wbig_gate2 = f1_mm
            for w_h, s_nm, src, dst in ((h1T_h, "sh1", u_buf, g_buf),
                                        (h2T_h, "sh2", g_buf, u_buf)):
                f2_mm, f2_ev = {}, {}
                for o in range(FCH):
                    par = o % 2
                    bk = (0, 1) if par == 0 else (2, 3)
                    if o >= 2:
                        b.wait("pool", cPE, f2_mm[o - 2])
                    else:
                        b.wait("pool", cPE, wbig_gate2[FCH - 2 + o]
                               if len(wbig_gate2) >= FCH else wbig_gate2[6 + o])
                    b.op("pool", lambda e, o=o, w_h=w_h: e.dma_start(
                        out=wbig[:, o % 2, :, :],
                        in_=w_h[:, o * 128:(o + 1) * 128].rearrange(
                            "(j p) n -> p j n", p=128)), inc=dPL, inc_n=16)
                    b.wait("pe", dPL)
                    if o >= 2:
                        b.wait("pe", cACT, f2_ev[o - 2])
                    for c in range(FCH):
                        for tt in range(2):
                            b.op("pe", lambda e, o=o, c=c, tt=tt, bk=bk, src=src:
                                 e.matmul(
                                     pb[bk[tt]][:], wbig[:, o % 2, c, :],
                                     src[:, c, tt * 512:(tt + 1) * 512],
                                     start=(c == 0), stop=(c == FCH - 1)),
                                 inc=cPE if (c == FCH - 1 and tt == 1) else None)
                    f2_mm[o] = cPE.n
                    b.wait("act", cPE, f2_mm[o])
                    b.op("act", lambda e, o=o, bk=bk, dst=dst, s_nm=s_nm:
                         e.activation(
                             out=dst[:, o, 0:1024], in_=pq[bk[0] // 2][:],
                             func=AF.Gelu, scale=sc[s_nm][:, o:o + 1]), inc=cACT)
                    f2_ev[o] = cACT.n
                wbig_gate2 = f2_mm

            f4_mm, f4_ev = {}, {}
            for cc in range(CCH):
                par = cc % 2
                bk = (0, 1) if par == 0 else (2, 3)
                if cc >= 2:
                    b.wait("pool", cPE, f4_mm[cc - 2])
                else:
                    b.wait("pool", cPE, wbig_gate2[FCH - 2 + cc])
                b.op("pool", lambda e, cc=cc: e.dma_start(
                    out=wbig[:, cc % 2, :, :],
                    in_=w3T_h[:, cc * 128:(cc + 1) * 128].rearrange(
                        "(j p) n -> p j n", p=128)), inc=dPL, inc_n=16)
                b.wait("pe", dPL)
                if cc >= 2:
                    b.wait("pe", cACT, f4_ev[cc - 2])
                for c in range(FCH):
                    for tt in range(2):
                        b.op("pe", lambda e, cc=cc, c=c, tt=tt, bk=bk: e.matmul(
                            pb[bk[tt]][:], wbig[:, cc % 2, c, :],
                            u_buf[:, c, tt * 512:(tt + 1) * 512],
                            start=(c == 0), stop=(c == FCH - 1)),
                            inc=cPE if (c == FCH - 1 and tt == 1) else None)
                f4_mm[cc] = cPE.n
                b.wait("act", cPE, f4_mm[cc])
                b.op("act", lambda e, cc=cc, bk=bk: e.activation(
                    out=y_buf[:, cc, 0:1024], in_=pq[bk[0] // 2][:],
                    func=AF.Identity, scale=sc["s3"][:, cc:cc + 1]), inc=cACT)
                f4_ev[cc] = cACT.n
            y_done = cACT.n
            wbig_gate = {6 + i: f4_mm[6 + i] for i in range(2)}
            prev_f1 = f1_mm
            prev_f4ev = f4_ev[CCH - 1]

            for tci in range(8):
                tc = pss * 8 + tci
                sx, sa = xt[tci % 2], xt[2]
                if tc - 1 in out_done:
                    b.wait("sp", sOUT, out_done[tc - 1])
                b.wait("sp", cDVE)
                b.wait("sp", cPOOL)
                b.wait("sp", dSP, e_park[tc])
                b.op("sp", lambda e, tc=tc, sx=sx: e.dma_start(
                    out=sx[:], in_=x_h[tc * 128:(tc + 1) * 128, :]),
                    inc=dSP, inc_n=16)
                b.op("sp", lambda e, tc=tc, sa=sa: e.dma_start(
                    out=sa[:], in_=acc_park[tc * 128:(tc + 1) * 128, :]),
                    inc=dSP, inc_n=16)
                ldd = dSP.n
                b.wait("pe", cACT, y_done)
                fb = (6, 7) if tci % 2 == 0 else (4, 5)
                if tc - 2 in f5_add:
                    b.wait("pe", cDVE, f5_add[tc - 2])
                for j in range(CCH):
                    bank = fb[j // 4]
                    b.op("pe", lambda e, j=j, tci=tci, bank=bank: e.transpose(
                        pb[bank][:, (j % 4) * 128:(j % 4 + 1) * 128],
                        y_buf[:, j, tci * 128:(tci + 1) * 128], idf[:]),
                        inc=cPE if j == CCH - 1 else None)
                tr_snap = cPE.n
                b.wait("pool", dSP, ldd)
                b.op("pool", lambda e, sx=sx, sa=sa: e.tensor_tensor(
                    sx[:], sx[:], sa[:], ALU.add), inc=cPOOL)
                b.wait("dve", cPE, tr_snap)
                b.wait("dve", cPOOL)
                b.op("dve", lambda e, sx=sx, fb=fb: e.tensor_tensor(
                    sx[:, 0:512], sx[:, 0:512], pb[fb[0]][:], ALU.add))
                b.op("dve", lambda e, sx=sx, fb=fb: e.tensor_tensor(
                    sx[:, 512:1024], sx[:, 512:1024], pb[fb[1]][:], ALU.add),
                    inc=cDVE)
                f5_add[tc] = cDVE.n
                b.wait("sp", cDVE)
                b.op("sp", lambda e, tc=tc, sx=sx: e.dma_start(
                    out=out_h[tc * 128:(tc + 1) * 128, :], in_=sx[:]),
                    inc=sOUT, inc_n=16)
                out_done[tc] = sOUT.n
                f5_dve = cDVE.n

        b.wait("sp", dSP)
        b.wait("sp", sOUT)
        return finish(nc, b)


def finish(nc, b):
    with ExitStack() as _fs:
        with nc.Block() as block:
            @block.sync
            def _(e):
                for f in b.q["sp"]:
                    f(e)

            @block.gpsimd
            def _(e):
                for f in b.q["pool"]:
                    f(e)

            @block.tensor
            def _(e):
                for f in b.q["pe"]:
                    f(e)

            @block.scalar
            def _(e):
                for f in b.q["act"]:
                    f(e)

            @block.vector
            def _(e):
                for f in b.q["dve"]:
                    f(e)
    return nc


_NC_CACHE = {}


def kernel(**inputs):
    def f32(a):
        return np.ascontiguousarray(np.asarray(a, dtype=np.float32))

    x = f32(inputs["L0_attn_x"])
    shared = {
        "qKT": f32(np.asarray(inputs["L0_attn_qK"]).T),
        "kKT": f32(np.asarray(inputs["L0_attn_kK"]).T),
        "vKT": f32(np.asarray(inputs["L0_attn_vK"]).T),
        "pKT": f32(np.asarray(inputs["L0_attn_pK"]).T),
        "qV": f32(inputs["L0_attn_qV"]),
        "kV": f32(inputs["L0_attn_kV"]),
        "vV": f32(inputs["L0_attn_vV"]),
        "pV": f32(inputs["L0_attn_pV"]),
        "fKT": f32(np.asarray(inputs["L0_attn_fK"]).T),
        "fV": f32(inputs["L0_attn_fV"]),
        "w1T": f32(np.asarray(inputs["L0_ffn_w1_w"]).T),
        "w2T": f32(np.asarray(inputs["L0_ffn_w2_w"]).T),
        "h1T": f32(np.asarray(inputs["L0_ffn_h1_w"]).T),
        "h2T": f32(np.asarray(inputs["L0_ffn_h2_w"]).T),
        "w3T": f32(np.asarray(inputs["L0_ffn_w3_w"]).T),
        "s1": f32(np.asarray(inputs["L0_ffn_w1_s"]).reshape(FCH, 128).T),
        "s2": f32(np.asarray(inputs["L0_ffn_w2_s"]).reshape(FCH, 128).T),
        "sh1": f32(np.asarray(inputs["L0_ffn_h1_s"]).reshape(FCH, 128).T),
        "sh2": f32(np.asarray(inputs["L0_ffn_h2_s"]).reshape(FCH, 128).T),
        "s3": f32(np.asarray(inputs["L0_ffn_w3_s"]).reshape(CCH, 128).T),
        "idb": np.eye(128, dtype=ml_dtypes.bfloat16),
        "idf": np.eye(128, dtype=np.float32),
    }
    m = np.zeros((4, 128, 512), dtype=ml_dtypes.bfloat16)
    i = np.arange(128)[:, None]
    j = np.arange(512)[None, :]
    for r in range(4):
        m[r] = (j >= i + r * 128).astype(ml_dtypes.bfloat16)
    shared["mask"] = m

    if "nc" not in _NC_CACHE:
        _NC_CACHE["nc"] = build_nc()
    nc = _NC_CACHE["nc"]
    in_maps = [dict(shared, x=np.ascontiguousarray(x[b])) for b in range(8)]
    res = run_bass_kernel_spmd(nc, in_maps, core_ids=list(range(8)))
    return np.stack([np.asarray(res.results[i]["out"], dtype=np.float32)
                     for i in range(8)], axis=0)


# version 11
# speedup vs baseline: 1.0084x; 1.0084x over previous
import numpy as np
import ml_dtypes
import concourse.bass as bass
import concourse.mybir as mybir
from concourse.bass_utils import run_bass_kernel_spmd
from contextlib import ExitStack

F32 = mybir.dt.float32
BF16 = mybir.dt.bfloat16
U8 = mybir.dt.uint8
AF = mybir.ActivationFunctionType
ALU = mybir.AluOpType

T, C, H, D, NF = 2048, 1024, 16, 64, 4096
TCH, CCH, FCH = T // 128, C // 128, NF // 128
EPS = 1e-5
KB = 1024

ENGS = ("sp", "pool", "pe", "act", "dve")


class Sem:
    __slots__ = ("h", "n", "name")

    def __init__(self, h, name):
        self.h, self.n, self.name = h, 0, name


class B:

    def __init__(self, nc, stack):
        self.nc = nc
        self.stack = stack
        self.q = {e: [] for e in ENGS}
        self.waited = {e: {} for e in ENGS}

    def sem(self, name):
        return Sem(self.stack.enter_context(self.nc.semaphore(name)), name)

    def wait(self, eng, sem, val=None):
        v = sem.n if val is None else val
        assert v <= sem.n, f"wait on future value {v} > {sem.n} of {sem.name}"
        if v <= 0:
            return
        w = self.waited[eng]
        if w.get(sem.name, -1) >= v:
            return
        w[sem.name] = v
        self.q[eng].append(lambda e, s=sem, vv=v: e.wait_ge(s.h, vv))

    def op(self, eng, fn, inc=None, inc_n=1):
        if inc is not None:
            inc.n += inc_n

        def run(e, fn=fn, inc=inc, inc_n=inc_n):
            ins = fn(e)
            if inc is not None:
                ins.then_inc(inc.h, inc_n)

        self.q[eng].append(run)


def build_nc(stop=8, dbg=False):
    nc = bass.Bass("TRN2", target_bir_lowering=False)
    dt_ = nc.dram_tensor
    if dbg:
        dbg_fm = dt_("dbg_fm", [128, CCH * T], BF16, kind="ExternalOutput").ap()
        dbg_fm2 = dt_("dbg_fm2", [128, CCH * T], BF16, kind="ExternalOutput").ap()
        dbg_tm = dt_("dbg_tm", [T, C], F32, kind="ExternalOutput").ap()
        dbg_bf = dt_("dbg_bf", [128, TCH * 1040], BF16, kind="ExternalOutput").ap()
    x_h = dt_("x", [T, C], F32, kind="ExternalInput").ap()
    KT = {k: dt_(f"{k}T", [C, C], F32, kind="ExternalInput").ap()
          for k in ("qK", "kK", "vK", "pK")}
    V = {k: dt_(k, [C, C], F32, kind="ExternalInput").ap()
         for k in ("qV", "kV", "vV", "pV")}
    fKT_h = dt_("fKT", [C, NF], F32, kind="ExternalInput").ap()
    fV_h = dt_("fV", [NF, C], F32, kind="ExternalInput").ap()
    w1T_h = dt_("w1T", [C, NF], F32, kind="ExternalInput").ap()
    w2T_h = dt_("w2T", [C, NF], F32, kind="ExternalInput").ap()
    h1T_h = dt_("h1T", [NF, NF], F32, kind="ExternalInput").ap()
    h2T_h = dt_("h2T", [NF, NF], F32, kind="ExternalInput").ap()
    w3T_h = dt_("w3T", [NF, C], F32, kind="ExternalInput").ap()
    sc_h = {nm: dt_(nm, [128, FCH if nm != "s3" else CCH], F32,
                    kind="ExternalInput").ap()
            for nm in ("s1", "s2", "sh1", "sh2", "s3")}
    idb_h = dt_("idb", [128, 128], BF16, kind="ExternalInput").ap()
    idf_h = dt_("idf", [128, 128], F32, kind="ExternalInput").ap()
    mask_h = dt_("mask", [128, 896], BF16, kind="ExternalInput").ap()
    xn_park = dt_("xn_park", [T, C], F32, kind="Internal").ap()
    xnfm_park = dt_("xnfm_park", [128, CCH * T], BF16, kind="Internal").ap()
    a1_park = dt_("a1_park", [T, C], F32, kind="Internal").ap()
    acc_park = dt_("acc_park", [T, C], F32, kind="Internal").ap()
    out_h = dt_("out", [T, C], F32, kind="ExternalOutput").ap()

    st = ExitStack()
    with st:
        en = st.enter_context
        idb = en(nc.sbuf_tensor("idb_sb", [128, 128], BF16))
        idf = en(nc.sbuf_tensor("idf_sb", [128, 128], F32))
        masks = en(nc.sbuf_tensor("masks_sb", [128, 896], BF16))
        sc = {nm: en(nc.sbuf_tensor(nm + "_sb", [128, FCH if nm != "s3" else CCH], F32))
              for nm in ("s1", "s2", "sh1", "sh2", "s3")}
        _lnt = [en(nc.sbuf_tensor(f"lns{i}", [128, 20], F32)) for i in range(2)]
        LNS = [{"stats": t[:, 0:12], "mv": t[:, 12:14], "veps": t[:, 14:15],
                "sqv": t[:, 15:16], "rstd": t[:, 16:17], "nbias": t[:, 17:18]}
               for t in _lnt]
        rcp = en(nc.sbuf_tensor("rcp", [128, 4], F32))
        epsc = en(nc.sbuf_tensor("epsc", [128, 1], F32))
        xt = [en(nc.sbuf_tensor(f"xt{i}", [128, C], F32)) for i in range(3)]
        sbf = [en(nc.sbuf_tensor(f"sbf{i}", [128, C], BF16)) for i in range(2)]
        sf32 = en(nc.sbuf_tensor("sf32", [128, C], F32))
        wlog = en(nc.sbuf_tensor("wlog", [128, 4, CCH, 128], BF16))
        wbig = en(nc.sbuf_tensor("wbig", [128, 2, FCH, 128], BF16))

        R0, R1, R2, R3, R4 = 0, 33280, 66048, 98816, 131648
        AR_BYTES = 164416
        arena = en(nc.sbuf_tensor("arena", [128, AR_BYTES], U8))
        arb = arena.bitcast(BF16)
        arf = arena.bitcast(F32)

        def bfr(off_b, n):
            return arb[:, off_b // 2: off_b // 2 + n]

        def f32r(off_b, n):
            return arf[:, off_b // 4: off_b // 4 + n]

        h_fm = bfr(R0, CCH * T).rearrange("p (j t) -> p j t", t=T)
        vext = bfr(R0, TCH * 1040).rearrange("p (i e) -> p i e", e=1040)
        a_tf = bfr(R0, FCH * 1024).rearrange("p (j t) -> p j t", t=1024)
        u_buf = bfr(R0, FCH * 1024).rearrange("p (j t) -> p j t", t=1024)
        xn_fm = bfr(R1, CCH * T).rearrange("p (j t) -> p j t", t=T)
        vbufB = bfr(R1, CCH * 1025).rearrange("p (j e) -> p j e", e=1025)
        o_tm = bfr(R1, TCH * C).rearrange("p (i c) -> p i c", c=C)
        vbufD = bfr(R1, CCH * 1025).rearrange("p (j e) -> p j e", e=1025)
        a_t = bfr(R2, CCH * T).rearrange("p (j t) -> p j t", t=T)
        p_t = [bfr(R2 + 16384 * i, 16 * 512).rearrange("p (k t) -> p k t", t=512)
               for i in range(2)]
        fvbuf = bfr(R2, FCH * 1025).rearrange("p (j e) -> p j e", e=1025)
        g_buf = bfr(R2, FCH * 1024).rearrange("p (j t) -> p j t", t=1024)
        q_fm = bfr(R3, CCH * T).rearrange("p (j t) -> p j t", t=T)
        o_fm = bfr(R3, CCH * T).rearrange("p (j t) -> p j t", t=T)
        k_fm = bfr(R4, CCH * T).rearrange("p (j t) -> p j t", t=T)
        lnafm = bfr(R4, CCH * T).rearrange("p (j t) -> p j t", t=T)
        xn_ffn = bfr(R4, CCH * 1024).rearrange("p (j t) -> p j t", t=1024)
        y_buf = f32r(R2, CCH * 1024).rearrange("p (j t) -> p j t", t=1024)
        sf32b = xt[1]

        pq = [en(nc.psum_tensor(f"pq{i}", [128, 1024], F32)) for i in range(4)]
        pb = [pq[i // 2][:, (i % 2) * 512:(i % 2 + 1) * 512] for i in range(8)]
        pqb = [t.bitcast(BF16) for t in pq]
        pbb = [pqb[i // 2][:, (i % 2) * 1024:(i % 2 + 1) * 1024] for i in range(8)]

        b = B(nc, st)
        cPE = b.sem("cPE")
        cACT = b.sem("cACT")
        cDVE = b.sem("cDVE")
        cPOOL = b.sem("cPOOL")
        dSP = b.sem("dSP")
        dPL = b.sem("dPL")
        sOUT = b.sem("sOUT")

        ln_last_act = [0, 0]

        def ln_tile(src, dst, st=0, src_dve=None, src_act=None, dst_pe=None):
            S = LNS[st]
            b.wait("dve", cACT, ln_last_act[st] if src_act is None else
                   max(ln_last_act[st], src_act))
            if src_act is None and src_dve is None:
                b.wait("dve", cDVE)
            elif src_dve is not None:
                b.wait("dve", cDVE, src_dve)
            b.op("dve", lambda e: e.bn_stats(S["stats"][:, 0:6], src[:, 0:512]))
            b.op("dve", lambda e: e.bn_stats(S["stats"][:, 6:12], src[:, 512:1024]),
                 inc=cDVE)
            b.wait("dve", cDVE)
            b.op("dve", lambda e: e.bn_aggr(S["mv"][:], S["stats"][:]), inc=cDVE)
            b.wait("act", cDVE)
            b.op("act", lambda e: e.activation(out=S["sqv"][:], in_=S["mv"][:, 1:2],
                                               func=AF.Ln, bias=epsc[:]), inc=cACT)
            b.wait("act", cACT)
            b.op("act", lambda e: e.activation(out=S["rstd"][:], in_=S["sqv"][:],
                                               func=AF.Exp, scale=-0.5), inc=cACT)
            b.wait("dve", cACT)
            b.op("dve", lambda e: e.scalar_tensor_tensor(
                S["nbias"][:], S["mv"][:, 0:1], -1.0, S["rstd"][:], ALU.mult,
                ALU.mult), inc=cDVE)
            b.wait("act", cDVE)
            if dst_pe is None:
                b.wait("act", cPE)
            else:
                b.wait("act", cPE, dst_pe)
            b.op("act", lambda e: e.activation(
                out=dst, in_=src, func=AF.Identity, scale=S["rstd"][:],
                bias=S["nbias"][:]), inc=cACT)
            ln_last_act[st] = cACT.n

        def transpose_tm_to_fm(src_bf, dst_fm, tc, bank):
            b.wait("pe", cDVE)
            b.wait("pe", cACT)
            pt = pbb[bank]
            for j in range(CCH):
                b.op("pe", lambda e, j=j: e.transpose(
                    pt[:, j * 128:(j + 1) * 128],
                    src_bf[:, j * 128:(j + 1) * 128], idb[:]),
                    inc=cPE if j == CCH - 1 else None)
            b.wait("dve", cPE)
            src_v = pt[:, 0:1024].rearrange("p (j t) -> p j t", t=128)
            b.op("dve", lambda e: e.tensor_copy(
                dst_fm[:, :, tc * 128:(tc + 1) * 128], src_v), inc=cDVE)

        def logits_stage(KT_ap, rhs_fm, nch, dst_at, tt_n, tt_off, scale, banks0,
                         banks1):
            b.wait("pe", cDVE)
            mm_done, exp_done = {}, {}
            for n in range(nch):
                par = n % 2
                bks = banks0 if par == 0 else banks1
                if n >= 3:
                    b.wait("pool", cPE, mm_done[n - 3])
                else:
                    b.wait("pool", cPE)
                b.op("pool", lambda e, n=n: e.dma_start(
                    out=wlog[:, n % 4, :, :],
                    in_=KT_ap[:, n * 128:(n + 1) * 128].rearrange(
                        "(j p) n -> p j n", p=128)), inc=dPL, inc_n=16)
                b.wait("pe", dPL)
                if n >= 2:
                    b.wait("pe", cACT, exp_done[n - 2])
                for c in range(CCH):
                    for tt in range(tt_n):
                        b.op("pe", lambda e, n=n, c=c, tt=tt, bks=bks: e.matmul(
                            pb[bks[tt]][:], wlog[:, n % 4, c, :],
                            rhs_fm[:, c, tt_off + tt * 512: tt_off + (tt + 1) * 512],
                            start=(c == 0), stop=(c == CCH - 1)),
                            inc=cPE if (c == CCH - 1 and tt == tt_n - 1) else None)
                mm_done[n] = cPE.n
                b.wait("act", cPE, mm_done[n])
                for tp in range(tt_n // 2):
                    qi = bks[2 * tp] // 2
                    b.op("act", lambda e, n=n, tp=tp, qi=qi: e.activation(
                        out=dst_at[:, n, tt_off + tp * 1024: tt_off + (tp + 1) * 1024],
                        in_=pq[qi][:], func=AF.Exp, scale=scale), inc=cACT)
                exp_done[n] = cACT.n

        def ov_stage(vbuf, src_at, nch, dst_kind, dst, tc_range, t_base, c0b, c1b,
                     ssb, extra=None):
            b.wait("pe", cDVE)
            evac_done, stg_ready, tr_pe, tr_ev = {}, {}, {}, {}
            ex_ready = {}
            pend = None

            def emit_transpose(i, tcv):
                stg = sbf[i % 2]
                b.wait("pe", cDVE, stg_ready[i])
                if i - 1 in tr_ev:
                    b.wait("pe", cDVE, tr_ev[i - 1])
                pt = pbb[7]
                for j in range(CCH):
                    b.op("pe", lambda e, j=j, pt=pt, stg=stg: e.transpose(
                        pt[:, j * 128:(j + 1) * 128],
                        stg[:, j * 128:(j + 1) * 128], idb[:]),
                        inc=cPE if j == CCH - 1 else None)
                tr_pe[i] = cPE.n
                b.wait("dve", cPE, tr_pe[i])
                src_v = pt[:, 0:1024].rearrange("p (j t) -> p j t", t=128)
                b.op("dve", lambda e, tcv=tcv, src_v=src_v: e.tensor_copy(
                    dst[:, :, tcv * 128:(tcv + 1) * 128], src_v), inc=cDVE)
                tr_ev[i] = cDVE.n

            for idx, tc in enumerate(tc_range):
                par = idx % 2
                bc0, bc1, bs = c0b[par], c1b[par], ssb[par]
                if idx >= 2:
                    sem2, val2 = evac_done[idx - 2]
                    b.wait("pe", sem2, val2)
                lt = (tc - t_base) * 128
                for n in range(nch):
                    lhs = src_at[:, n, lt:lt + 128]
                    b.op("pe", lambda e, lhs=lhs, n=n, bc0=bc0: e.matmul(
                        pb[bc0][:], lhs, vbuf[:, n, 0:512],
                        start=(n == 0), stop=(n == nch - 1)))
                    b.op("pe", lambda e, lhs=lhs, n=n, bc1=bc1: e.matmul(
                        pb[bc1][:], lhs, vbuf[:, n, 512:1024],
                        start=(n == 0), stop=(n == nch - 1)))
                    b.op("pe", lambda e, lhs=lhs, n=n, bs=bs: e.matmul(
                        pb[bs][:, 0:1], lhs, vbuf[:, n, 1024:1025],
                        start=(n == 0), stop=(n == nch - 1)),
                        inc=cPE if n == nch - 1 else None)
                mm = cPE.n
                if pend is not None and dst_kind == "qk":
                    emit_transpose(*pend)
                    pend = None
                b.wait("dve", cPE, mm)
                if idx >= 2 and dst_kind == "f32":
                    sem2, val2 = evac_done[idx - 2]
                    b.wait("dve", sem2, val2)
                b.op("dve", lambda e, bs=bs, par=par: e.reciprocal(
                    rcp[:, par:par + 1], pb[bs][:, 0:1]), inc=cDVE)
                rcp_done = cDVE.n
                if dst_kind == "qk":
                    stg = sbf[par]
                    b.wait("dve", cDVE, rcp_done)
                    b.op("dve", lambda e, par=par, stg=stg: e.tensor_scalar(
                        stg[:, 0:1024], pq[par][:], rcp[:, par:par + 1], None,
                        ALU.mult), inc=cDVE)
                    evac_done[idx] = (cDVE, cDVE.n)
                    stg_ready[idx] = cDVE.n
                    pend = (idx, tc)
                elif dst_kind == "vext":
                    vv = vext[:, tc, 0:1040].rearrange("p (h e) -> p h e", e=65)
                    b.wait("dve", cDVE, rcp_done)
                    b.op("dve", lambda e, par=par, vv=vv: e.tensor_scalar(
                        vv[:, :, 0:64],
                        pq[par][:].rearrange("p (h d) -> p h d", d=64),
                        rcp[:, par:par + 1], None, ALU.mult), inc=cDVE)
                    evac_done[idx] = (cDVE, cDVE.n)
                else:
                    sg = sf32 if par == 0 else sf32b
                    b.wait("act", cDVE, rcp_done)
                    b.wait("act", cPE, mm)
                    b.op("act", lambda e, par=par, sg=sg: e.activation(
                        out=sg[:, 0:1024], in_=pq[par][:], func=AF.Identity,
                        scale=rcp[:, par:par + 1]), inc=cACT)
                    evac_done[idx] = (cACT, cACT.n)
                    if pend is not None:
                        extra(pend[1], sf32 if pend[0] % 2 == 0 else sf32b,
                              ex_ready[pend[0]])
                    ex_ready[idx] = cACT.n
                    pend = (idx, tc)
            if pend is not None:
                if dst_kind == "qk":
                    emit_transpose(*pend)
                else:
                    extra(pend[1], sf32 if pend[0] % 2 == 0 else sf32b,
                          ex_ready[pend[0]])

        def transpose_tm_to_fm_from_dve(src_bf, dst_fm, tc, bank=7):
            b.wait("pe", cDVE)
            pt = pbb[bank]
            for j in range(CCH):
                b.op("pe", lambda e, j=j: e.transpose(
                    pt[:, j * 128:(j + 1) * 128],
                    src_bf[:, j * 128:(j + 1) * 128], idb[:]),
                    inc=cPE if j == CCH - 1 else None)
            b.wait("dve", cPE)
            src_v = pt[:, 0:1024].rearrange("p (j t) -> p j t", t=128)
            b.op("dve", lambda e: e.tensor_copy(
                dst_fm[:, :, tc * 128:(tc + 1) * 128], src_v), inc=cDVE)

        b.op("sp", lambda e: e.dma_start(out=idb[:], in_=idb_h), inc=dSP, inc_n=16)
        b.op("sp", lambda e: e.dma_start(out=idf[:], in_=idf_h), inc=dSP, inc_n=16)
        b.op("sp", lambda e: e.dma_start(out=masks[:], in_=mask_h),
             inc=dSP, inc_n=16)
        for nm in ("s1", "s2", "sh1", "sh2", "s3"):
            b.op("sp", lambda e, nm=nm: e.dma_start(out=sc[nm][:], in_=sc_h[nm]),
                 inc=dSP, inc_n=16)
        consts_loaded = dSP.n
        b.op("pool", lambda e: e.memset(epsc[:], EPS), inc=cPOOL)
        b.wait("act", cPOOL)
        b.wait("dve", cPOOL)

        b.op("pool", lambda e: e.dma_start(
            out=qkw[:, :, :],
            in_=KT["qK"].rearrange("(j p) n -> p j n", p=128)), inc=dPL, inc_n=16)
        qkw_d = dPL.n
        xn_stage = [sf32, f32r(R3, C)]
        qkw = bfr(R3 + 4096, CCH * 1024).rearrange("p (j n) -> p j n", n=1024)
        ab_mm, ab_ex = {}, {}
        a_done = {}
        park_done = {}
        a_info = {}
        for tc in range(TCH):
            slot = xt[tc % 2]
            if tc >= 2:
                ai = a_info[tc - 2]
                b.wait("sp", cDVE, ai["ln1_stats"])
                b.wait("sp", cACT, ai["ln1_norm"])
            b.op("sp", lambda e, tc=tc, s=slot: e.dma_start(
                out=s[:], in_=x_h[tc * 128:(tc + 1) * 128, :]), inc=dSP, inc_n=16)
            ld = dSP.n
            xs = xn_stage[tc % 2]
            if tc >= 2:
                ai = a_info[tc - 2]
                b.wait("act", dSP, park_done[tc - 2])
                b.wait("act", cDVE, max(ai["ln2_stats"], ai["bf_copy"]))
            b.wait("dve", dSP, ld)
            ln_tile(slot[:], xs[:, 0:C], st=tc % 2)
            ln1_norm = cACT.n
            ln1_stats = cDVE.n
            b.wait("sp", cACT, ln1_norm)
            b.op("sp", lambda e, tc=tc, xs=xs: e.dma_start(
                out=xn_park[tc * 128:(tc + 1) * 128, :], in_=xs[:, 0:C]),
                inc=dSP, inc_n=16)
            park_done[tc] = dSP.n
            b.wait("dve", cACT, ln1_norm)
            if tc >= 1:
                b.wait("dve", cPE, a_info[tc - 1]["tr_xn"])
            b.op("dve", lambda e, xs=xs: e.tensor_copy(sbf[0][:], xs[:, 0:C]),
                 inc=cDVE)
            bf_copy = cDVE.n
            if tc >= 1:
                b.wait("act", cPE, a_info[tc - 1]["tr_h"])
            ln_tile(xs[:, 0:C], sbf[1][:], st=tc % 2, src_act=ln1_norm)
            ln2_norm = cACT.n
            ln2_stats = cDVE.n
            b.wait("pe", cDVE, bf_copy)
            if tc >= 1:
                b.wait("pe", cDVE, a_info[tc - 1]["ev_xn"])
            for j in range(CCH):
                b.op("pe", lambda e, j=j: e.transpose(
                    pbb[6][:, j * 128:(j + 1) * 128],
                    sbf[0][:, j * 128:(j + 1) * 128], idb[:]),
                    inc=cPE if j == CCH - 1 else None)
            tr_xn = cPE.n
            b.wait("dve", cPE, tr_xn)
            b.op("dve", lambda e, tc=tc: e.tensor_copy(
                xn_fm[:, :, tc * 128:(tc + 1) * 128],
                pbb[6][:, 0:1024].rearrange("p (j t) -> p j t", t=128)), inc=cDVE)
            ev_xn = cDVE.n
            b.wait("pe", cACT, ln2_norm)
            if tc >= 1:
                b.wait("pe", cDVE, a_info[tc - 1]["ev_h"])
            for j in range(CCH):
                b.op("pe", lambda e, j=j: e.transpose(
                    pbb[7][:, j * 128:(j + 1) * 128],
                    sbf[1][:, j * 128:(j + 1) * 128], idb[:]),
                    inc=cPE if j == CCH - 1 else None)
            tr_h = cPE.n
            b.wait("dve", cPE, tr_h)
            b.op("dve", lambda e, tc=tc: e.tensor_copy(
                h_fm[:, :, tc * 128:(tc + 1) * 128],
                pbb[7][:, 0:1024].rearrange("p (j t) -> p j t", t=128)), inc=cDVE)
            ev_h = cDVE.n
            a_info[tc] = dict(ln1_stats=ln1_stats, ln1_norm=ln1_norm,
                              ln2_stats=ln2_stats, bf_copy=bf_copy,
                              tr_xn=tr_xn, tr_h=tr_h, ev_xn=ev_xn, ev_h=ev_h)
            if tc % 4 == 3:
                tt = tc // 4
                b.wait("pe", dPL, qkw_d)
                b.wait("pe", cDVE, ev_h)
                for n in range(CCH):
                    gi = tt * CCH + n
                    if gi >= 4:
                        b.wait("pe", cACT, ab_ex[gi - 4])
                    for c in range(CCH):
                        b.op("pe", lambda e, gi=gi, n=n, c=c, tt=tt: e.matmul(
                            pb[gi % 4][:], qkw[:, c, n * 128:(n + 1) * 128],
                            h_fm[:, c, tt * 512:(tt + 1) * 512],
                            start=(c == 0), stop=(c == CCH - 1)),
                            inc=cPE if c == CCH - 1 else None)
                    ab_mm[gi] = cPE.n
                    if gi >= 2:
                        b.wait("act", cPE, ab_mm[gi - 2])
                        b.op("act", lambda e, gj=gi - 2: e.activation(
                            out=a_t[:, gj % CCH, (gj // CCH) * 512:
                                    (gj // CCH + 1) * 512],
                            in_=pb[gj % 4][:], func=AF.Exp, scale=1 / 32.0),
                            inc=cACT)
                        ab_ex[gi - 2] = cACT.n
                if tt == 3:
                    for gj in (30, 31):
                        b.wait("act", cPE, ab_mm[gj])
                        b.op("act", lambda e, gj=gj: e.activation(
                            out=a_t[:, gj % CCH, (gj // CCH) * 512:
                                    (gj // CCH + 1) * 512],
                            in_=pb[gj % 4][:], func=AF.Exp, scale=1 / 32.0),
                            inc=cACT)
                        ab_ex[gj] = cACT.n
        b.wait("sp", cDVE)
        b.op("sp", lambda e: e.dma_start(
            out=xnfm_park, in_=xn_fm.rearrange("p j t -> p (j t)")),
            inc=dSP, inc_n=16)
        xnfm_parked = dSP.n

        if stop <= 1:
            if dbg:
                b.wait("sp", cDVE)
                b.op("sp", lambda e: e.dma_start(
                    out=dbg_fm, in_=h_fm.rearrange("p j t -> p (j t)")),
                    inc=sOUT, inc_n=16)
                b.op("sp", lambda e: e.dma_start(
                    out=dbg_fm2, in_=xn_fm.rearrange("p j t -> p (j t)")),
                    inc=sOUT, inc_n=16)
            b.wait("sp", dSP)
            b.wait("sp", sOUT)
            return finish(nc, b)
        OVB = ((0, 2), (1, 3), (4, 5))

        b.wait("pool", dSP, xnfm_parked)
        b.op("pool", lambda e: e.dma_start(
            out=vbufB[:, :, 0:1024],
            in_=V["qV"].rearrange("(j p) c -> p j c", p=128)), inc=dPL, inc_n=16)
        b.op("pool", lambda e: e.memset(vbufB[:, :, 1024:1025], 1.0), inc=cPOOL)
        b.wait("pe", dPL)
        b.wait("pe", cPOOL)
        b.wait("pe", cACT)
        ov_stage(vbufB, a_t, CCH, "qk", q_fm, range(TCH), 0, *OVB)

        if stop <= 2:
            if dbg:
                b.wait("sp", cDVE)
                b.op("sp", lambda e: e.dma_start(
                    out=dbg_fm, in_=q_fm.rearrange("p j t -> p (j t)")),
                    inc=sOUT, inc_n=16)
                b.op("sp", lambda e: e.dma_start(
                    out=dbg_fm2, in_=a_t.rearrange("p j t -> p (j t)")),
                    inc=sOUT, inc_n=16)
            b.wait("sp", dSP)
            b.wait("sp", sOUT)
            return finish(nc, b)
        logits_stage(KT["kK"], h_fm, CCH, a_t, 4, 0, 1 / 32.0,
                     [0, 1, 2, 3], [4, 5, 6, 7])
        b.wait("pool", cPE)
        b.op("pool", lambda e: e.dma_start(
            out=vbufB[:, :, 0:1024],
            in_=V["kV"].rearrange("(j p) c -> p j c", p=128)), inc=dPL, inc_n=16)
        b.wait("pe", dPL)
        b.wait("pe", cACT)
        ov_stage(vbufB, a_t, CCH, "qk", k_fm, range(TCH), 0, *OVB)

        if stop <= 3:
            if dbg:
                b.wait("sp", cDVE)
                b.op("sp", lambda e: e.dma_start(
                    out=dbg_fm, in_=k_fm.rearrange("p j t -> p (j t)")),
                    inc=sOUT, inc_n=16)
            b.wait("sp", dSP)
            b.wait("sp", sOUT)
            return finish(nc, b)
        logits_stage(KT["vK"], h_fm, CCH, a_t, 4, 0, 1 / 32.0,
                     [0, 1, 2, 3], [4, 5, 6, 7])
        b.wait("pool", cPE)
        b.op("pool", lambda e: e.dma_start(
            out=vbufB[:, :, 0:1024],
            in_=V["vV"].rearrange("(j p) c -> p j c", p=128)), inc=dPL, inc_n=16)
        b.op("pool", lambda e: e.memset(
            vext[:, :, 0:1040].rearrange("p i (h e) -> p i h e", e=65)[:, :, :, 64:65],
            1.0), inc=cPOOL)
        b.wait("pe", dPL)
        b.wait("pe", cPOOL)
        b.wait("pe", cACT)
        ov_stage(vbufB, a_t, CCH, "vext", None, range(TCH), 0, *OVB)

        if stop <= 4:
            if dbg:
                b.wait("sp", cDVE)
                b.op("sp", lambda e: e.dma_start(
                    out=dbg_bf, in_=vext.rearrange("p i e -> p (i e)")),
                    inc=sOUT, inc_n=16)
            b.wait("sp", dSP)
            b.wait("sp", sOUT)
            return finish(nc, b)
        p_t2 = bfr(R2, 16 * 1024).rearrange("p (k t) -> p k t", t=1024)
        b.wait("pe", dSP, consts_loaded)
        b.wait("pe", cDVE)
        exp_par = {0: 0, 1: 0}
        o_evac = {}
        gidx = 0
        for hp in range(CCH):
            hA, hB = 2 * hp, 2 * hp + 1
            for qt in range(4):
                kmax = 4 * qt + 4
                mask_snap = cDVE.n
                for kt in range(kmax):
                    par = kt % 2
                    bA, bB = 2 * par, 2 * par + 1
                    b.wait("pe", cACT, exp_par[par])
                    b.op("pe", lambda e, hp=hp, kt=kt, qt=qt, bA=bA: e.matmul(
                        pb[bA][:],
                        k_fm[0:64, hp, kt * 128:(kt + 1) * 128],
                        q_fm[0:64, hp, qt * 512:(qt + 1) * 512],
                        start=True, stop=True, tile_position=(0, 0)))
                    b.op("pe", lambda e, hp=hp, kt=kt, qt=qt, bB=bB: e.matmul(
                        pb[bB][:],
                        k_fm[64:128, hp, kt * 128:(kt + 1) * 128],
                        q_fm[64:128, hp, qt * 512:(qt + 1) * 512],
                        start=True, stop=True, tile_position=(64, 0)), inc=cPE)
                    b.wait("act", cPE, cPE.n)
                    b.op("act", lambda e, kt=kt, par=par: e.activation(
                        out=p_t2[:, kt, :], in_=pq[par][:], func=AF.Exp,
                        scale=0.125), inc=cACT)
                    exp_par[par] = cACT.n
                    if kt >= 4 * qt:
                        r = kt - 4 * qt
                        off = 384 - 128 * r
                        b.wait("dve", cACT, cACT.n)
                        b.op("dve", lambda e, kt=kt, off=off: e.tensor_tensor(
                            p_t2[:, kt, 0:512], p_t2[:, kt, 0:512],
                            masks[:, off:off + 512], ALU.mult))
                        b.op("dve", lambda e, kt=kt, off=off: e.tensor_tensor(
                            p_t2[:, kt, 512:1024], p_t2[:, kt, 512:1024],
                            masks[:, off:off + 512], ALU.mult), inc=cDVE)
                        mask_snap = cDVE.n
                exp_all = max(exp_par.values())
                b.wait("pe", cACT, exp_all)
                b.wait("pe", cDVE, mask_snap)
                for h2 in range(2):
                    hh = hA if h2 == 0 else hB
                    for sub in range(4):
                        bank = 4 + gidx % 4
                        if gidx - 4 in o_evac:
                            b.wait("pe", cDVE, o_evac[gidx - 4])
                        for kt in range(kmax):
                            b.op("pe", lambda e, h2=h2, kt=kt, sub=sub, hh=hh,
                                 bank=bank, kmax=kmax: e.matmul(
                                     pb[bank][:, 0:65],
                                     p_t2[:, kt, h2 * 512 + sub * 128:
                                          h2 * 512 + (sub + 1) * 128],
                                     vext[:, kt, hh * 65:hh * 65 + 65],
                                     start=(kt == 0), stop=(kt == kmax - 1)),
                                 inc=cPE if kt == kmax - 1 else None)
                        b.wait("dve", cPE, cPE.n)
                        rs = 2 + gidx % 2
                        b.op("dve", lambda e, bank=bank, rs=rs: e.reciprocal(
                            rcp[:, rs:rs + 1], pb[bank][:, 64:65]), inc=cDVE)
                        b.wait("dve", cDVE)
                        b.op("dve", lambda e, bank=bank, qt=qt, sub=sub, hh=hh,
                             rs=rs: e.tensor_scalar(
                                 o_tm[:, qt * 4 + sub, hh * 64:(hh + 1) * 64],
                                 pb[bank][:, 0:64], rcp[:, rs:rs + 1], None,
                                 ALU.mult), inc=cDVE)
                        o_evac[gidx] = cDVE.n
                        gidx += 1

        if stop <= 5:
            if dbg:
                b.wait("sp", cDVE)
                b.op("sp", lambda e: e.dma_start(
                    out=dbg_bf[:, 0:TCH * 1024],
                    in_=o_tm.rearrange("p i c -> p (i c)")), inc=sOUT, inc_n=16)
            b.wait("sp", dSP)
            b.wait("sp", sOUT)
            return finish(nc, b)
        for tc in range(TCH):
            transpose_tm_to_fm_from_dve(o_tm[:, tc, :], o_fm, tc, bank=7)
        logits_stage(KT["pK"], o_fm, CCH, a_t, 4, 0, 1 / 32.0,
                     [0, 1, 2, 3], [4, 5, 6, 7])
        b.wait("pool", cPE)
        b.op("pool", lambda e: e.dma_start(
            out=vbufD[:, :, 0:1024],
            in_=V["pV"].rearrange("(j p) c -> p j c", p=128)), inc=dPL, inc_n=16)
        b.op("pool", lambda e: e.memset(vbufD[:, :, 1024:1025], 1.0), inc=cPOOL)
        b.wait("pe", dPL)
        b.wait("pe", cPOOL)
        b.wait("pe", cACT)

        d_park = {}
        d_lnr = {}
        d_pend = {}
        d_trev = {}

        def d_tr_flush(tc):
            snap = d_pend.pop(tc)
            stg = sbf[tc % 2]
            b.wait("pe", cACT, snap)
            if tc - 1 in d_trev:
                b.wait("pe", cDVE, d_trev[tc - 1])
            pt = pbb[7]
            for j in range(CCH):
                b.op("pe", lambda e, j=j, pt=pt, stg=stg: e.transpose(
                    pt[:, j * 128:(j + 1) * 128],
                    stg[:, j * 128:(j + 1) * 128], idb[:]),
                    inc=cPE if j == CCH - 1 else None)
            b.wait("dve", cPE, cPE.n)
            src_v = pt[:, 0:1024].rearrange("p (j t) -> p j t", t=128)
            b.op("dve", lambda e, tc=tc, src_v=src_v: e.tensor_copy(
                lnafm[:, :, tc * 128:(tc + 1) * 128], src_v), inc=cDVE)
            d_trev[tc] = cDVE.n

        def d_extra(tc, sg, sg_act):
            slot = xt[2] if tc % 2 == 0 else xt[0]
            if tc >= 2:
                b.wait("sp", dSP, d_park[tc - 2])
                b.wait("sp", cACT, d_lnr[tc - 2])
            b.wait("sp", cPOOL)
            b.wait("sp", dSP, park_done[tc])
            b.op("sp", lambda e, tc=tc, s=slot: e.dma_start(
                out=s[:], in_=xn_park[tc * 128:(tc + 1) * 128, :]),
                inc=dSP, inc_n=16)
            ld = dSP.n
            b.wait("pool", dSP, ld)
            b.wait("pool", cACT, sg_act)
            b.op("pool", lambda e, s=slot, sg=sg: e.tensor_tensor(
                s[:], s[:], sg[:], ALU.add), inc=cPOOL)
            add_snap = cPOOL.n
            b.wait("sp", cPOOL, add_snap)
            b.op("sp", lambda e, tc=tc, s=slot: e.dma_start(
                out=a1_park[tc * 128:(tc + 1) * 128, :], in_=s[:]),
                inc=dSP, inc_n=16)
            d_park[tc] = dSP.n
            b.wait("dve", cPOOL, add_snap)
            ln_tile(slot[:], sbf[tc % 2][:], st=tc % 2)
            d_lnr[tc] = cACT.n
            d_pend[tc] = cACT.n
            if tc - 1 in d_pend:
                d_tr_flush(tc - 1)

        ov_stage(vbufD, a_t, CCH, "f32", None, range(TCH), 0, *OVB, extra=d_extra)
        for _tc in sorted(d_pend):
            d_tr_flush(_tc)

        if stop <= 6:
            if dbg:
                b.wait("sp", cDVE)
                b.op("sp", lambda e: e.dma_start(
                    out=dbg_fm, in_=lnafm.rearrange("p j t -> p (j t)")),
                    inc=sOUT, inc_n=16)
                b.op("sp", lambda e: e.dma_start(
                    out=dbg_fm2, in_=o_fm.rearrange("p j t -> p (j t)")),
                    inc=sOUT, inc_n=16)
                b.wait("sp", dSP, d_park[TCH - 1])
                b.op("sp", lambda e: e.dma_start(out=dbg_tm, in_=a1_park),
                    inc=sOUT, inc_n=16)
            b.wait("sp", dSP)
            b.wait("sp", sOUT)
            return finish(nc, b)
        d_pe_done = cPE.n
        e_park = {}
        fv_d = None
        for pss in range(2):
            t_off = pss * 1024
            b.wait("pe", cDVE)
            e1_mm, e1_ex = {}, {}
            for n in range(FCH):
                par = n % 2
                bks = (0, 1) if par == 0 else (2, 3)
                if n >= 3:
                    b.wait("pool", cPE, e1_mm[n - 3])
                else:
                    b.wait("pool", cPE)
                b.op("pool", lambda e, n=n: e.dma_start(
                    out=wlog[:, n % 4, :, :],
                    in_=fKT_h[:, n * 128:(n + 1) * 128].rearrange(
                        "(j p) n -> p j n", p=128)), inc=dPL, inc_n=16)
                if pss == 0 and n < 8:
                    b.op("pool", lambda e, n=n: e.dma_start(
                        out=fvbuf[:, 4 * n:4 * n + 4, 0:1024],
                        in_=fV_h[n * 512:(n + 1) * 512, :].rearrange(
                            "(j p) c -> p j c", p=128)), inc=dPL, inc_n=16)
                    if n == 7:
                        b.op("pool", lambda e: e.memset(
                            fvbuf[:, :, 1024:1025], 1.0), inc=cPOOL)
                        fv_d, fv_c = dPL.n, cPOOL.n
                b.wait("pe", dPL)
                if n >= 2:
                    b.wait("pe", cACT, e1_ex[n - 2])
                for c in range(CCH):
                    for tt in range(2):
                        b.op("pe", lambda e, n=n, c=c, tt=tt, bks=bks, t_off=t_off:
                             e.matmul(
                                 pb[bks[tt]][:], wlog[:, n % 4, c, :],
                                 lnafm[:, c, t_off + tt * 512:t_off + (tt + 1) * 512],
                                 start=(c == 0), stop=(c == CCH - 1)),
                             inc=cPE if (c == CCH - 1 and tt == 1) else None)
                e1_mm[n] = cPE.n
                b.wait("act", cPE, e1_mm[n])
                b.op("act", lambda e, n=n, bks=bks: e.activation(
                    out=a_tf[:, n, 0:1024],
                    in_=pq[bks[0] // 2][:], func=AF.Exp, scale=1 / 32.0), inc=cACT)
                e1_ex[n] = cACT.n
            b.wait("pe", dPL, fv_d)
            b.wait("pe", cPOOL, fv_c)
            b.wait("pe", cACT)

            def e_extra(tc, sg, sg_act):
                slot = xt[2]
                if tc - 1 in e_park:
                    b.wait("sp", dSP, e_park[tc - 1])
                b.wait("sp", cPOOL)
                b.wait("sp", dSP, d_park[tc])
                b.op("sp", lambda e, tc=tc, s=slot: e.dma_start(
                    out=s[:], in_=a1_park[tc * 128:(tc + 1) * 128, :]),
                    inc=dSP, inc_n=16)
                ld = dSP.n
                b.wait("pool", dSP, ld)
                b.wait("pool", cACT, sg_act)
                b.op("pool", lambda e, s=slot, sg=sg: e.tensor_tensor(
                    s[:], s[:], sg[:], ALU.add), inc=cPOOL)
                b.wait("sp", cPOOL)
                b.op("sp", lambda e, tc=tc, s=slot: e.dma_start(
                    out=acc_park[tc * 128:(tc + 1) * 128, :], in_=s[:]),
                    inc=dSP, inc_n=16)
                e_park[tc] = dSP.n

            ov_stage(fvbuf, a_tf, FCH, "f32", None,
                     range(pss * 8, pss * 8 + 8), pss * 8, *OVB, extra=e_extra)
            e1_last = e1_mm

        if stop <= 7:
            if dbg:
                b.wait("sp", dSP, e_park[TCH - 1])
                b.op("sp", lambda e: e.dma_start(out=dbg_tm, in_=acc_park),
                    inc=sOUT, inc_n=16)
            b.wait("sp", dSP)
            b.wait("sp", sOUT)
            return finish(nc, b)
        out_done = {}
        f5_add = {}
        wbig_gate = None
        prev_f1 = None
        prev_f4ev = None
        f5_dve = 0
        for pss in range(2):
            t_off = pss * 1024
            if pss == 0:
                b.wait("sp", cPE, e1_last[FCH - 1])
            else:
                b.wait("sp", cPE, prev_f1[FCH - 1])
            b.wait("sp", dSP, xnfm_parked)
            b.op("sp", lambda e, t_off=t_off: e.dma_start(
                out=xn_ffn[:, :, :],
                in_=xnfm_park.rearrange("p (j t) -> p j t", t=T)[
                    :, :, t_off:t_off + 1024]),
                inc=dSP, inc_n=16)
            xn_ldd = dSP.n

            f1_mm, f1_ev = {}, {}
            for o in range(FCH):
                par = o % 2
                bks = (0, 1, 2, 3) if par == 0 else (4, 5, 6, 7)
                if o >= 3:
                    b.wait("pool", cPE, f1_mm[o - 3])
                elif pss == 0:
                    b.wait("pool", cPE, e1_last[FCH - 3 + o])
                else:
                    b.wait("pool", cPE, prev_f1[FCH - 3 + o])
                b.op("pool", lambda e, o=o: e.dma_start(
                    out=wlog[:, o % 4, :, :],
                    in_=w1T_h[:, o * 128:(o + 1) * 128].rearrange(
                        "(j p) n -> p j n", p=128)), inc=dPL, inc_n=16)
                if o >= 2:
                    b.wait("pool", cPE, f1_mm[o - 2])
                elif wbig_gate is not None:
                    b.wait("pool", cPE, wbig_gate[6 + o])
                b.op("pool", lambda e, o=o: e.dma_start(
                    out=wbig[:, o % 2, 0:CCH, :],
                    in_=w2T_h[:, o * 128:(o + 1) * 128].rearrange(
                        "(j p) n -> p j n", p=128)), inc=dPL, inc_n=16)
                b.wait("pe", dPL)
                b.wait("pe", dSP, xn_ldd)
                if o >= 2:
                    b.wait("pe", cDVE, f1_ev[o - 2])
                elif pss == 1:
                    if o == 0:
                        b.wait("pe", cACT, prev_f4ev)
                    else:
                        b.wait("pe", cDVE, f5_dve)
                else:
                    b.wait("pe", cDVE)
                for c in range(CCH):
                    for tt in range(2):
                        b.op("pe", lambda e, o=o, c=c, tt=tt, bks=bks: e.matmul(
                            pb[bks[tt]][:], wlog[:, o % 4, c, :],
                            xn_ffn[:, c, tt * 512:(tt + 1) * 512],
                            start=(c == 0), stop=(c == CCH - 1)))
                for c in range(CCH):
                    for tt in range(2):
                        b.op("pe", lambda e, o=o, c=c, tt=tt, bks=bks: e.matmul(
                            pb[bks[2 + tt]][:], wbig[:, o % 2, c, :],
                            xn_ffn[:, c, tt * 512:(tt + 1) * 512],
                            start=(c == 0), stop=(c == CCH - 1)),
                            inc=cPE if (c == CCH - 1 and tt == 1) else None)
                f1_mm[o] = cPE.n
                b.wait("act", cPE, f1_mm[o])
                b.wait("act", cDVE)
                b.op("act", lambda e, o=o, bks=bks: e.activation(
                    out=u_buf[:, o, 0:1024], in_=pq[bks[0] // 2][:], func=AF.Silu,
                    scale=sc["s1"][:, o:o + 1]), inc=cACT)
                b.op("act", lambda e, o=o, bks=bks: e.activation(
                    out=sbf[0][:, 0:1024], in_=pq[bks[2] // 2][:], func=AF.Identity,
                    scale=sc["s2"][:, o:o + 1]), inc=cACT)
                b.wait("dve", cACT)
                b.op("dve", lambda e, o=o: e.tensor_tensor(
                    u_buf[:, o, :], u_buf[:, o, :], sbf[0][:], ALU.mult), inc=cDVE)
                f1_ev[o] = cDVE.n

            wbig_gate2 = f1_mm
            for w_h, s_nm, src, dst in ((h1T_h, "sh1", u_buf, g_buf),
                                        (h2T_h, "sh2", g_buf, u_buf)):
                f2_mm, f2_ev = {}, {}
                for o in range(FCH):
                    par = o % 2
                    bk = (0, 1) if par == 0 else (2, 3)
                    if o >= 2:
                        b.wait("pool", cPE, f2_mm[o - 2])
                    else:
                        b.wait("pool", cPE, wbig_gate2[FCH - 2 + o]
                               if len(wbig_gate2) >= FCH else wbig_gate2[6 + o])
                    b.op("pool", lambda e, o=o, w_h=w_h: e.dma_start(
                        out=wbig[:, o % 2, :, :],
                        in_=w_h[:, o * 128:(o + 1) * 128].rearrange(
                            "(j p) n -> p j n", p=128)), inc=dPL, inc_n=16)
                    b.wait("pe", dPL)
                    if o >= 2:
                        b.wait("pe", cACT, f2_ev[o - 2])
                    for c in range(FCH):
                        for tt in range(2):
                            b.op("pe", lambda e, o=o, c=c, tt=tt, bk=bk, src=src:
                                 e.matmul(
                                     pb[bk[tt]][:], wbig[:, o % 2, c, :],
                                     src[:, c, tt * 512:(tt + 1) * 512],
                                     start=(c == 0), stop=(c == FCH - 1)),
                                 inc=cPE if (c == FCH - 1 and tt == 1) else None)
                    f2_mm[o] = cPE.n
                    b.wait("act", cPE, f2_mm[o])
                    b.op("act", lambda e, o=o, bk=bk, dst=dst, s_nm=s_nm:
                         e.activation(
                             out=dst[:, o, 0:1024], in_=pq[bk[0] // 2][:],
                             func=AF.Gelu, scale=sc[s_nm][:, o:o + 1]), inc=cACT)
                    f2_ev[o] = cACT.n
                wbig_gate2 = f2_mm

            f4_mm, f4_ev = {}, {}
            for cc in range(CCH):
                par = cc % 2
                bk = (0, 1) if par == 0 else (2, 3)
                if cc >= 2:
                    b.wait("pool", cPE, f4_mm[cc - 2])
                else:
                    b.wait("pool", cPE, wbig_gate2[FCH - 2 + cc])
                b.op("pool", lambda e, cc=cc: e.dma_start(
                    out=wbig[:, cc % 2, :, :],
                    in_=w3T_h[:, cc * 128:(cc + 1) * 128].rearrange(
                        "(j p) n -> p j n", p=128)), inc=dPL, inc_n=16)
                b.wait("pe", dPL)
                if cc >= 2:
                    b.wait("pe", cACT, f4_ev[cc - 2])
                for c in range(FCH):
                    for tt in range(2):
                        b.op("pe", lambda e, cc=cc, c=c, tt=tt, bk=bk: e.matmul(
                            pb[bk[tt]][:], wbig[:, cc % 2, c, :],
                            u_buf[:, c, tt * 512:(tt + 1) * 512],
                            start=(c == 0), stop=(c == FCH - 1)),
                            inc=cPE if (c == FCH - 1 and tt == 1) else None)
                f4_mm[cc] = cPE.n
                b.wait("act", cPE, f4_mm[cc])
                b.op("act", lambda e, cc=cc, bk=bk: e.activation(
                    out=y_buf[:, cc, 0:1024], in_=pq[bk[0] // 2][:],
                    func=AF.Identity, scale=sc["s3"][:, cc:cc + 1]), inc=cACT)
                f4_ev[cc] = cACT.n
            y_done = cACT.n
            wbig_gate = {6 + i: f4_mm[6 + i] for i in range(2)}
            prev_f1 = f1_mm
            prev_f4ev = f4_ev[CCH - 1]

            for tci in range(8):
                tc = pss * 8 + tci
                sx, sa = xt[tci % 2], xt[2]
                if tc - 1 in out_done:
                    b.wait("sp", sOUT, out_done[tc - 1])
                b.wait("sp", cDVE)
                b.wait("sp", cPOOL)
                b.wait("sp", dSP, e_park[tc])
                b.op("sp", lambda e, tc=tc, sx=sx: e.dma_start(
                    out=sx[:], in_=x_h[tc * 128:(tc + 1) * 128, :]),
                    inc=dSP, inc_n=16)
                b.op("sp", lambda e, tc=tc, sa=sa: e.dma_start(
                    out=sa[:], in_=acc_park[tc * 128:(tc + 1) * 128, :]),
                    inc=dSP, inc_n=16)
                ldd = dSP.n
                b.wait("pe", cACT, y_done)
                fb = (6, 7)
                if tc - 1 in f5_add:
                    b.wait("pe", cDVE, f5_add[tc - 1])
                for j in range(CCH):
                    bank = fb[j // 4]
                    b.op("pe", lambda e, j=j, tci=tci, bank=bank: e.transpose(
                        pb[bank][:, (j % 4) * 128:(j % 4 + 1) * 128],
                        y_buf[:, j, tci * 128:(tci + 1) * 128], idf[:]),
                        inc=cPE if j == CCH - 1 else None)
                tr_snap = cPE.n
                b.wait("pool", dSP, ldd)
                b.op("pool", lambda e, sx=sx, sa=sa: e.tensor_tensor(
                    sx[:], sx[:], sa[:], ALU.add), inc=cPOOL)
                b.wait("dve", cPE, tr_snap)
                b.wait("dve", cPOOL)
                b.op("dve", lambda e, sx=sx, fb=fb: e.tensor_tensor(
                    sx[:, 0:512], sx[:, 0:512], pb[fb[0]][:], ALU.add))
                b.op("dve", lambda e, sx=sx, fb=fb: e.tensor_tensor(
                    sx[:, 512:1024], sx[:, 512:1024], pb[fb[1]][:], ALU.add),
                    inc=cDVE)
                f5_add[tc] = cDVE.n
                b.wait("sp", cDVE)
                b.op("sp", lambda e, tc=tc, sx=sx: e.dma_start(
                    out=out_h[tc * 128:(tc + 1) * 128, :], in_=sx[:]),
                    inc=sOUT, inc_n=16)
                out_done[tc] = sOUT.n
                f5_dve = cDVE.n

        b.wait("sp", dSP)
        b.wait("sp", sOUT)
        return finish(nc, b)


def finish(nc, b):
    with ExitStack() as _fs:
        with nc.Block() as block:
            @block.sync
            def _(e):
                for f in b.q["sp"]:
                    f(e)

            @block.gpsimd
            def _(e):
                for f in b.q["pool"]:
                    f(e)

            @block.tensor
            def _(e):
                for f in b.q["pe"]:
                    f(e)

            @block.scalar
            def _(e):
                for f in b.q["act"]:
                    f(e)

            @block.vector
            def _(e):
                for f in b.q["dve"]:
                    f(e)
    return nc


_NC_CACHE = {}


def kernel(**inputs):
    def f32(a):
        return np.ascontiguousarray(np.asarray(a, dtype=np.float32))

    x = f32(inputs["L0_attn_x"])
    shared = {
        "qKT": f32(np.asarray(inputs["L0_attn_qK"]).T),
        "kKT": f32(np.asarray(inputs["L0_attn_kK"]).T),
        "vKT": f32(np.asarray(inputs["L0_attn_vK"]).T),
        "pKT": f32(np.asarray(inputs["L0_attn_pK"]).T),
        "qV": f32(inputs["L0_attn_qV"]),
        "kV": f32(inputs["L0_attn_kV"]),
        "vV": f32(inputs["L0_attn_vV"]),
        "pV": f32(inputs["L0_attn_pV"]),
        "fKT": f32(np.asarray(inputs["L0_attn_fK"]).T),
        "fV": f32(inputs["L0_attn_fV"]),
        "w1T": f32(np.asarray(inputs["L0_ffn_w1_w"]).T),
        "w2T": f32(np.asarray(inputs["L0_ffn_w2_w"]).T),
        "h1T": f32(np.asarray(inputs["L0_ffn_h1_w"]).T),
        "h2T": f32(np.asarray(inputs["L0_ffn_h2_w"]).T),
        "w3T": f32(np.asarray(inputs["L0_ffn_w3_w"]).T),
        "s1": f32(np.asarray(inputs["L0_ffn_w1_s"]).reshape(FCH, 128).T),
        "s2": f32(np.asarray(inputs["L0_ffn_w2_s"]).reshape(FCH, 128).T),
        "sh1": f32(np.asarray(inputs["L0_ffn_h1_s"]).reshape(FCH, 128).T),
        "sh2": f32(np.asarray(inputs["L0_ffn_h2_s"]).reshape(FCH, 128).T),
        "s3": f32(np.asarray(inputs["L0_ffn_w3_s"]).reshape(CCH, 128).T),
        "idb": np.eye(128, dtype=ml_dtypes.bfloat16),
        "idf": np.eye(128, dtype=np.float32),
    }
    m = np.zeros((4, 128, 512), dtype=ml_dtypes.bfloat16)
    i = np.arange(128)[:, None]
    j = np.arange(512)[None, :]
    for r in range(4):
        m[r] = (j >= i + r * 128).astype(ml_dtypes.bfloat16)
    shared["mask"] = m

    if "nc" not in _NC_CACHE:
        _NC_CACHE["nc"] = build_nc()
    nc = _NC_CACHE["nc"]
    in_maps = [dict(shared, x=np.ascontiguousarray(x[b])) for b in range(8)]
    res = run_bass_kernel_spmd(nc, in_maps, core_ids=list(range(8)))
    return np.stack([np.asarray(res.results[i]["out"], dtype=np.float32)
                     for i in range(8)], axis=0)


# version 12
# speedup vs baseline: 1.0313x; 1.0227x over previous
import numpy as np
import ml_dtypes
import concourse.bass as bass
import concourse.mybir as mybir
from concourse.bass_utils import run_bass_kernel_spmd
from contextlib import ExitStack

F32 = mybir.dt.float32
BF16 = mybir.dt.bfloat16
U8 = mybir.dt.uint8
AF = mybir.ActivationFunctionType
ALU = mybir.AluOpType

T, C, H, D, NF = 2048, 1024, 16, 64, 4096
TCH, CCH, FCH = T // 128, C // 128, NF // 128
EPS = 1e-5
KB = 1024

ENGS = ("sp", "pool", "pe", "act", "dve")


class Sem:
    __slots__ = ("h", "n", "name")

    def __init__(self, h, name):
        self.h, self.n, self.name = h, 0, name


class B:

    def __init__(self, nc, stack):
        self.nc = nc
        self.stack = stack
        self.q = {e: [] for e in ENGS}
        self.waited = {e: {} for e in ENGS}

    def sem(self, name):
        return Sem(self.stack.enter_context(self.nc.semaphore(name)), name)

    def wait(self, eng, sem, val=None):
        v = sem.n if val is None else val
        assert v <= sem.n, f"wait on future value {v} > {sem.n} of {sem.name}"
        if v <= 0:
            return
        w = self.waited[eng]
        if w.get(sem.name, -1) >= v:
            return
        w[sem.name] = v
        self.q[eng].append(lambda e, s=sem, vv=v: e.wait_ge(s.h, vv))

    def op(self, eng, fn, inc=None, inc_n=1):
        if inc is not None:
            inc.n += inc_n

        def run(e, fn=fn, inc=inc, inc_n=inc_n):
            ins = fn(e)
            if inc is not None:
                ins.then_inc(inc.h, inc_n)

        self.q[eng].append(run)


def build_nc(stop=8, dbg=False):
    nc = bass.Bass("TRN2", target_bir_lowering=False)
    dt_ = nc.dram_tensor
    if dbg:
        dbg_fm = dt_("dbg_fm", [128, CCH * T], BF16, kind="ExternalOutput").ap()
        dbg_fm2 = dt_("dbg_fm2", [128, CCH * T], BF16, kind="ExternalOutput").ap()
        dbg_tm = dt_("dbg_tm", [T, C], F32, kind="ExternalOutput").ap()
        dbg_bf = dt_("dbg_bf", [128, TCH * 1040], BF16, kind="ExternalOutput").ap()
    x_h = dt_("x", [T, C], F32, kind="ExternalInput").ap()
    KT = {k: dt_(f"{k}T", [C, C], F32, kind="ExternalInput").ap()
          for k in ("qK", "kK", "vK", "pK")}
    V = {k: dt_(k, [C, C], F32, kind="ExternalInput").ap()
         for k in ("qV", "kV", "vV", "pV")}
    fKT_h = dt_("fKT", [C, NF], F32, kind="ExternalInput").ap()
    fV_h = dt_("fV", [NF, C], F32, kind="ExternalInput").ap()
    w1T_h = dt_("w1T", [C, NF], F32, kind="ExternalInput").ap()
    w2T_h = dt_("w2T", [C, NF], F32, kind="ExternalInput").ap()
    h1T_h = dt_("h1T", [NF, NF], F32, kind="ExternalInput").ap()
    h2T_h = dt_("h2T", [NF, NF], F32, kind="ExternalInput").ap()
    w3T_h = dt_("w3T", [NF, C], F32, kind="ExternalInput").ap()
    sc_h = {nm: dt_(nm, [128, FCH if nm != "s3" else CCH], F32,
                    kind="ExternalInput").ap()
            for nm in ("s1", "s2", "sh1", "sh2", "s3")}
    idb_h = dt_("idb", [128, 128], BF16, kind="ExternalInput").ap()
    idf_h = dt_("idf", [128, 128], F32, kind="ExternalInput").ap()
    mask_h = dt_("mask", [128, 896], BF16, kind="ExternalInput").ap()
    xn_park = dt_("xn_park", [T, C], F32, kind="Internal").ap()
    xnfm_park = dt_("xnfm_park", [128, CCH * T], BF16, kind="Internal").ap()
    a1_park = dt_("a1_park", [T, C], F32, kind="Internal").ap()
    acc_park = dt_("acc_park", [T, C], F32, kind="Internal").ap()
    out_h = dt_("out", [T, C], F32, kind="ExternalOutput").ap()

    st = ExitStack()
    with st:
        en = st.enter_context
        idb = en(nc.sbuf_tensor("idb_sb", [128, 128], BF16))
        idf = en(nc.sbuf_tensor("idf_sb", [128, 128], F32))
        masks = en(nc.sbuf_tensor("masks_sb", [128, 896], BF16))
        sc = {nm: en(nc.sbuf_tensor(nm + "_sb", [128, FCH if nm != "s3" else CCH], F32))
              for nm in ("s1", "s2", "sh1", "sh2", "s3")}
        _lnt = [en(nc.sbuf_tensor(f"lns{i}", [128, 20], F32)) for i in range(2)]
        LNS = [{"stats": t[:, 0:12], "mv": t[:, 12:14], "veps": t[:, 14:15],
                "sqv": t[:, 15:16], "rstd": t[:, 16:17], "nbias": t[:, 17:18]}
               for t in _lnt]
        rcp = en(nc.sbuf_tensor("rcp", [128, 4], F32))
        epsc = _lnt[0][:, 18:19]
        onec = _lnt[0].bitcast(BF16)[:, 38:39]
        xt = [en(nc.sbuf_tensor(f"xt{i}", [128, C], F32)) for i in range(3)]
        sbf = [en(nc.sbuf_tensor(f"sbf{i}", [128, C], BF16)) for i in range(2)]
        sf32 = en(nc.sbuf_tensor("sf32", [128, C], F32))
        wlog = en(nc.sbuf_tensor("wlog", [128, 4, CCH, 128], BF16))
        wbig = en(nc.sbuf_tensor("wbig", [128, 2, FCH, 128], BF16))

        R0, R1, R2, R3, R4 = 0, 33280, 66048, 98816, 131648
        AR_BYTES = 164416
        arena = en(nc.sbuf_tensor("arena", [128, AR_BYTES], U8))
        arb = arena.bitcast(BF16)
        arf = arena.bitcast(F32)

        def bfr(off_b, n):
            return arb[:, off_b // 2: off_b // 2 + n]

        def f32r(off_b, n):
            return arf[:, off_b // 4: off_b // 4 + n]

        h_fm = bfr(R0, CCH * T).rearrange("p (j t) -> p j t", t=T)
        vext = bfr(R0, TCH * 1040).rearrange("p (i e) -> p i e", e=1040)
        a_tf = bfr(R0, FCH * 1024).rearrange("p (j t) -> p j t", t=1024)
        u_buf = bfr(R0, FCH * 1024).rearrange("p (j t) -> p j t", t=1024)
        xn_fm = bfr(R1, CCH * T).rearrange("p (j t) -> p j t", t=T)
        vbufB = [bfr(R1 + 16384 * i, CCH * 1024).rearrange(
            "p (j e) -> p j e", e=1024) for i in range(2)]
        o_tm = bfr(R1, TCH * C).rearrange("p (i c) -> p i c", c=C)
        vbufD = bfr(R1, CCH * 1024).rearrange("p (j e) -> p j e", e=1024)
        a_t = bfr(R2, CCH * T).rearrange("p (j t) -> p j t", t=T)
        p_t = [bfr(R2 + 16384 * i, 16 * 512).rearrange("p (k t) -> p k t", t=512)
               for i in range(2)]
        fvbuf = bfr(R2, FCH * 1024).rearrange("p (j e) -> p j e", e=1024)
        g_buf = bfr(R2, FCH * 1024).rearrange("p (j t) -> p j t", t=1024)
        q_fm = bfr(R3, CCH * T).rearrange("p (j t) -> p j t", t=T)
        o_fm = bfr(R3, CCH * T).rearrange("p (j t) -> p j t", t=T)
        k_fm = bfr(R4, CCH * T).rearrange("p (j t) -> p j t", t=T)
        lnafm = bfr(R4, CCH * T).rearrange("p (j t) -> p j t", t=T)
        xn_ffn = bfr(R4, CCH * 1024).rearrange("p (j t) -> p j t", t=1024)
        y_buf = f32r(R2, CCH * 1024).rearrange("p (j t) -> p j t", t=1024)
        sf32b = xt[1]

        pq = [en(nc.psum_tensor(f"pq{i}", [128, 1024], F32)) for i in range(4)]
        pb = [pq[i // 2][:, (i % 2) * 512:(i % 2 + 1) * 512] for i in range(8)]
        pqb = [t.bitcast(BF16) for t in pq]
        pbb = [pqb[i // 2][:, (i % 2) * 1024:(i % 2 + 1) * 1024] for i in range(8)]

        b = B(nc, st)
        cPE = b.sem("cPE")
        cACT = b.sem("cACT")
        cDVE = b.sem("cDVE")
        cPOOL = b.sem("cPOOL")
        dSP = b.sem("dSP")
        dPL = b.sem("dPL")
        sOUT = b.sem("sOUT")

        ln_last_act = [0, 0]

        def ln_tile(src, dst, st=0, src_dve=None, src_act=None, dst_pe=None):
            S = LNS[st]
            b.wait("dve", cACT, ln_last_act[st] if src_act is None else
                   max(ln_last_act[st], src_act))
            if src_act is None and src_dve is None:
                b.wait("dve", cDVE)
            elif src_dve is not None:
                b.wait("dve", cDVE, src_dve)
            b.op("dve", lambda e: e.bn_stats(S["stats"][:, 0:6], src[:, 0:512]))
            b.op("dve", lambda e: e.bn_stats(S["stats"][:, 6:12], src[:, 512:1024]),
                 inc=cDVE)
            b.wait("dve", cDVE)
            b.op("dve", lambda e: e.bn_aggr(S["mv"][:], S["stats"][:]), inc=cDVE)
            b.wait("act", cDVE)
            b.op("act", lambda e: e.activation(out=S["sqv"][:], in_=S["mv"][:, 1:2],
                                               func=AF.Ln, bias=epsc[:]), inc=cACT)
            b.wait("act", cACT)
            b.op("act", lambda e: e.activation(out=S["rstd"][:], in_=S["sqv"][:],
                                               func=AF.Exp, scale=-0.5), inc=cACT)
            b.wait("dve", cACT)
            b.op("dve", lambda e: e.scalar_tensor_tensor(
                S["nbias"][:], S["mv"][:, 0:1], -1.0, S["rstd"][:], ALU.mult,
                ALU.mult), inc=cDVE)
            b.wait("act", cDVE)
            if dst_pe is None:
                b.wait("act", cPE)
            else:
                b.wait("act", cPE, dst_pe)
            b.op("act", lambda e: e.activation(
                out=dst, in_=src, func=AF.Identity, scale=S["rstd"][:],
                bias=S["nbias"][:]), inc=cACT)
            ln_last_act[st] = cACT.n

        def transpose_tm_to_fm(src_bf, dst_fm, tc, bank):
            b.wait("pe", cDVE)
            b.wait("pe", cACT)
            pt = pbb[bank]
            for j in range(CCH):
                b.op("pe", lambda e, j=j: e.transpose(
                    pt[:, j * 128:(j + 1) * 128],
                    src_bf[:, j * 128:(j + 1) * 128], idb[:]),
                    inc=cPE if j == CCH - 1 else None)
            b.wait("dve", cPE)
            src_v = pt[:, 0:1024].rearrange("p (j t) -> p j t", t=128)
            b.op("dve", lambda e: e.tensor_copy(
                dst_fm[:, :, tc * 128:(tc + 1) * 128], src_v), inc=cDVE)

        def logits_stage(KT_ap, rhs_fm, nch, dst_at, tt_n, tt_off, scale, banks0,
                         banks1):
            b.wait("pe", cDVE)
            mm_done, exp_done = {}, {}
            for n in range(nch):
                par = n % 2
                bks = banks0 if par == 0 else banks1
                if n >= 3:
                    b.wait("pool", cPE, mm_done[n - 3])
                else:
                    b.wait("pool", cPE)
                b.op("pool", lambda e, n=n: e.dma_start(
                    out=wlog[:, n % 4, :, :],
                    in_=KT_ap[:, n * 128:(n + 1) * 128].rearrange(
                        "(j p) n -> p j n", p=128)), inc=dPL, inc_n=16)
                b.wait("pe", dPL)
                if n >= 2:
                    b.wait("pe", cACT, exp_done[n - 2])
                for c in range(CCH):
                    for tt in range(tt_n):
                        b.op("pe", lambda e, n=n, c=c, tt=tt, bks=bks: e.matmul(
                            pb[bks[tt]][:], wlog[:, n % 4, c, :],
                            rhs_fm[:, c, tt_off + tt * 512: tt_off + (tt + 1) * 512],
                            start=(c == 0), stop=(c == CCH - 1)),
                            inc=cPE if (c == CCH - 1 and tt == tt_n - 1) else None)
                mm_done[n] = cPE.n
                b.wait("act", cPE, mm_done[n])
                for tp in range(tt_n // 2):
                    qi = bks[2 * tp] // 2
                    b.op("act", lambda e, n=n, tp=tp, qi=qi: e.activation(
                        out=dst_at[:, n, tt_off + tp * 1024: tt_off + (tp + 1) * 1024],
                        in_=pq[qi][:], func=AF.Exp, scale=scale), inc=cACT)
                exp_done[n] = cACT.n

        def ov_stage(vbuf, src_at, nch, dst_kind, dst, tc_range, t_base, c0b, c1b,
                     ssb, extra=None):
            b.wait("pe", cDVE)
            evac_done, stg_ready, tr_pe, tr_ev = {}, {}, {}, {}
            ex_ready = {}
            pend = None

            def emit_transpose(i, tcv):
                stg = sbf[i % 2]
                b.wait("pe", cDVE, stg_ready[i])
                if i - 1 in tr_ev:
                    b.wait("pe", cDVE, tr_ev[i - 1])
                pt = pbb[7]
                for j in range(CCH):
                    b.op("pe", lambda e, j=j, pt=pt, stg=stg: e.transpose(
                        pt[:, j * 128:(j + 1) * 128],
                        stg[:, j * 128:(j + 1) * 128], idb[:]),
                        inc=cPE if j == CCH - 1 else None)
                tr_pe[i] = cPE.n
                b.wait("dve", cPE, tr_pe[i])
                src_v = pt[:, 0:1024].rearrange("p (j t) -> p j t", t=128)
                b.op("dve", lambda e, tcv=tcv, src_v=src_v: e.tensor_copy(
                    dst[:, :, tcv * 128:(tcv + 1) * 128], src_v), inc=cDVE)
                tr_ev[i] = cDVE.n

            for idx, tc in enumerate(tc_range):
                par = idx % 2
                bc0, bc1, bs = c0b[par], c1b[par], ssb[par]
                if idx >= 2:
                    sem2, val2 = evac_done[idx - 2]
                    b.wait("pe", sem2, val2)
                lt = (tc - t_base) * 128
                for n in range(nch):
                    lhs = src_at[:, n, lt:lt + 128]
                    b.op("pe", lambda e, lhs=lhs, n=n, bc0=bc0: e.matmul(
                        pb[bc0][:], lhs, vbuf[:, n, 0:512],
                        start=(n == 0), stop=(n == nch - 1)))
                    b.op("pe", lambda e, lhs=lhs, n=n, bc1=bc1: e.matmul(
                        pb[bc1][:], lhs, vbuf[:, n, 512:1024],
                        start=(n == 0), stop=(n == nch - 1)))
                    b.op("pe", lambda e, lhs=lhs, n=n, bs=bs: e.matmul(
                        pb[bs][:, 0:1], lhs, onec[:, 0:1],
                        start=(n == 0), stop=(n == nch - 1)),
                        inc=cPE if n == nch - 1 else None)
                mm = cPE.n
                if pend is not None and dst_kind == "qk":
                    emit_transpose(*pend)
                    pend = None
                b.wait("dve", cPE, mm)
                if idx >= 2 and dst_kind == "f32":
                    sem2, val2 = evac_done[idx - 2]
                    b.wait("dve", sem2, val2)
                b.op("dve", lambda e, bs=bs, par=par: e.reciprocal(
                    rcp[:, par:par + 1], pb[bs][:, 0:1]), inc=cDVE)
                rcp_done = cDVE.n
                if dst_kind == "qk":
                    stg = sbf[par]
                    b.wait("dve", cDVE, rcp_done)
                    b.op("dve", lambda e, par=par, stg=stg: e.tensor_scalar(
                        stg[:, 0:1024], pq[par][:], rcp[:, par:par + 1], None,
                        ALU.mult), inc=cDVE)
                    evac_done[idx] = (cDVE, cDVE.n)
                    stg_ready[idx] = cDVE.n
                    pend = (idx, tc)
                elif dst_kind == "vext":
                    vv = vext[:, tc, 0:1040].rearrange("p (h e) -> p h e", e=65)
                    b.wait("dve", cDVE, rcp_done)
                    b.op("dve", lambda e, par=par, vv=vv: e.tensor_scalar(
                        vv[:, :, 0:64],
                        pq[par][:].rearrange("p (h d) -> p h d", d=64),
                        rcp[:, par:par + 1], None, ALU.mult), inc=cDVE)
                    evac_done[idx] = (cDVE, cDVE.n)
                else:
                    sg = sf32 if par == 0 else sf32b
                    b.wait("act", cDVE, rcp_done)
                    b.wait("act", cPE, mm)
                    b.op("act", lambda e, par=par, sg=sg: e.activation(
                        out=sg[:, 0:1024], in_=pq[par][:], func=AF.Identity,
                        scale=rcp[:, par:par + 1]), inc=cACT)
                    evac_done[idx] = (cACT, cACT.n)
                    if pend is not None:
                        extra(pend[1], sf32 if pend[0] % 2 == 0 else sf32b,
                              ex_ready[pend[0]])
                    ex_ready[idx] = cACT.n
                    pend = (idx, tc)
            if pend is not None:
                if dst_kind == "qk":
                    emit_transpose(*pend)
                else:
                    extra(pend[1], sf32 if pend[0] % 2 == 0 else sf32b,
                          ex_ready[pend[0]])

        def transpose_tm_to_fm_from_dve(src_bf, dst_fm, tc, bank=7):
            b.wait("pe", cDVE)
            pt = pbb[bank]
            for j in range(CCH):
                b.op("pe", lambda e, j=j: e.transpose(
                    pt[:, j * 128:(j + 1) * 128],
                    src_bf[:, j * 128:(j + 1) * 128], idb[:]),
                    inc=cPE if j == CCH - 1 else None)
            b.wait("dve", cPE)
            src_v = pt[:, 0:1024].rearrange("p (j t) -> p j t", t=128)
            b.op("dve", lambda e: e.tensor_copy(
                dst_fm[:, :, tc * 128:(tc + 1) * 128], src_v), inc=cDVE)

        b.op("sp", lambda e: e.dma_start(out=idb[:], in_=idb_h), inc=dSP, inc_n=16)
        b.op("sp", lambda e: e.dma_start(out=idf[:], in_=idf_h), inc=dSP, inc_n=16)
        b.op("sp", lambda e: e.dma_start(out=masks[:], in_=mask_h),
             inc=dSP, inc_n=16)
        for nm in ("s1", "s2", "sh1", "sh2", "s3"):
            b.op("sp", lambda e, nm=nm: e.dma_start(out=sc[nm][:], in_=sc_h[nm]),
                 inc=dSP, inc_n=16)
        consts_loaded = dSP.n
        b.op("pool", lambda e: e.memset(epsc[:], EPS), inc=cPOOL)
        b.op("pool", lambda e: e.memset(onec[:], 1.0), inc=cPOOL)
        b.wait("act", cPOOL)
        b.wait("dve", cPOOL)

        b.op("pool", lambda e: e.dma_start(
            out=qkw[:, :, :],
            in_=KT["qK"].rearrange("(j p) n -> p j n", p=128)), inc=dPL, inc_n=16)
        qkw_d = dPL.n
        xn_stage = [sf32, f32r(R3, C)]
        qkw = bfr(R3 + 4096, CCH * 1024).rearrange("p (j n) -> p j n", n=1024)
        ab_mm, ab_ex = {}, {}
        a_done = {}
        park_done = {}
        a_info = {}
        for tc in range(TCH):
            slot = xt[tc % 2]
            if tc >= 2:
                ai = a_info[tc - 2]
                b.wait("sp", cDVE, ai["ln1_stats"])
                b.wait("sp", cACT, ai["ln1_norm"])
            b.op("sp", lambda e, tc=tc, s=slot: e.dma_start(
                out=s[:], in_=x_h[tc * 128:(tc + 1) * 128, :]), inc=dSP, inc_n=16)
            ld = dSP.n
            xs = xn_stage[tc % 2]
            if tc >= 2:
                ai = a_info[tc - 2]
                b.wait("act", dSP, park_done[tc - 2])
                b.wait("act", cDVE, max(ai["ln2_stats"], ai["bf_copy"]))
            b.wait("dve", dSP, ld)
            ln_tile(slot[:], xs[:, 0:C], st=tc % 2)
            ln1_norm = cACT.n
            ln1_stats = cDVE.n
            b.wait("sp", cACT, ln1_norm)
            b.op("sp", lambda e, tc=tc, xs=xs: e.dma_start(
                out=xn_park[tc * 128:(tc + 1) * 128, :], in_=xs[:, 0:C]),
                inc=dSP, inc_n=16)
            park_done[tc] = dSP.n
            b.wait("dve", cACT, ln1_norm)
            if tc >= 1:
                b.wait("dve", cPE, a_info[tc - 1]["tr_xn"])
            b.op("dve", lambda e, xs=xs: e.tensor_copy(sbf[0][:], xs[:, 0:C]),
                 inc=cDVE)
            bf_copy = cDVE.n
            if tc >= 1:
                b.wait("act", cPE, a_info[tc - 1]["tr_h"])
            ln_tile(xs[:, 0:C], sbf[1][:], st=tc % 2, src_act=ln1_norm)
            ln2_norm = cACT.n
            ln2_stats = cDVE.n
            b.wait("pe", cDVE, bf_copy)
            if tc >= 1:
                b.wait("pe", cDVE, a_info[tc - 1]["ev_xn"])
            for j in range(CCH):
                b.op("pe", lambda e, j=j: e.transpose(
                    pbb[6][:, j * 128:(j + 1) * 128],
                    sbf[0][:, j * 128:(j + 1) * 128], idb[:]),
                    inc=cPE if j == CCH - 1 else None)
            tr_xn = cPE.n
            b.wait("dve", cPE, tr_xn)
            b.op("dve", lambda e, tc=tc: e.tensor_copy(
                xn_fm[:, :, tc * 128:(tc + 1) * 128],
                pbb[6][:, 0:1024].rearrange("p (j t) -> p j t", t=128)), inc=cDVE)
            ev_xn = cDVE.n
            b.wait("pe", cACT, ln2_norm)
            if tc >= 1:
                b.wait("pe", cDVE, a_info[tc - 1]["ev_h"])
            for j in range(CCH):
                b.op("pe", lambda e, j=j: e.transpose(
                    pbb[7][:, j * 128:(j + 1) * 128],
                    sbf[1][:, j * 128:(j + 1) * 128], idb[:]),
                    inc=cPE if j == CCH - 1 else None)
            tr_h = cPE.n
            b.wait("dve", cPE, tr_h)
            b.op("dve", lambda e, tc=tc: e.tensor_copy(
                h_fm[:, :, tc * 128:(tc + 1) * 128],
                pbb[7][:, 0:1024].rearrange("p (j t) -> p j t", t=128)), inc=cDVE)
            ev_h = cDVE.n
            a_info[tc] = dict(ln1_stats=ln1_stats, ln1_norm=ln1_norm,
                              ln2_stats=ln2_stats, bf_copy=bf_copy,
                              tr_xn=tr_xn, tr_h=tr_h, ev_xn=ev_xn, ev_h=ev_h)
            if tc % 4 == 3:
                tt = tc // 4
                b.wait("pe", dPL, qkw_d)
                b.wait("pe", cDVE, ev_h)
                for n in range(CCH):
                    gi = tt * CCH + n
                    if gi >= 4:
                        b.wait("pe", cACT, ab_ex[gi - 4])
                    for c in range(CCH):
                        b.op("pe", lambda e, gi=gi, n=n, c=c, tt=tt: e.matmul(
                            pb[gi % 4][:], qkw[:, c, n * 128:(n + 1) * 128],
                            h_fm[:, c, tt * 512:(tt + 1) * 512],
                            start=(c == 0), stop=(c == CCH - 1)),
                            inc=cPE if c == CCH - 1 else None)
                    ab_mm[gi] = cPE.n
                    if gi >= 2:
                        b.wait("act", cPE, ab_mm[gi - 2])
                        b.op("act", lambda e, gj=gi - 2: e.activation(
                            out=a_t[:, gj % CCH, (gj // CCH) * 512:
                                    (gj // CCH + 1) * 512],
                            in_=pb[gj % 4][:], func=AF.Exp, scale=1 / 32.0),
                            inc=cACT)
                        ab_ex[gi - 2] = cACT.n
                if tt == 3:
                    for gj in (30, 31):
                        b.wait("act", cPE, ab_mm[gj])
                        b.op("act", lambda e, gj=gj: e.activation(
                            out=a_t[:, gj % CCH, (gj // CCH) * 512:
                                    (gj // CCH + 1) * 512],
                            in_=pb[gj % 4][:], func=AF.Exp, scale=1 / 32.0),
                            inc=cACT)
                        ab_ex[gj] = cACT.n
        b.wait("sp", cDVE)
        b.op("sp", lambda e: e.dma_start(
            out=xnfm_park, in_=xn_fm.rearrange("p j t -> p (j t)")),
            inc=dSP, inc_n=16)
        xnfm_parked = dSP.n

        if stop <= 1:
            if dbg:
                b.wait("sp", cDVE)
                b.op("sp", lambda e: e.dma_start(
                    out=dbg_fm, in_=h_fm.rearrange("p j t -> p (j t)")),
                    inc=sOUT, inc_n=16)
                b.op("sp", lambda e: e.dma_start(
                    out=dbg_fm2, in_=xn_fm.rearrange("p j t -> p (j t)")),
                    inc=sOUT, inc_n=16)
            b.wait("sp", dSP)
            b.wait("sp", sOUT)
            return finish(nc, b)
        OVB = ((0, 2), (1, 3), (4, 5))

        b.wait("pool", dSP, xnfm_parked)
        b.op("pool", lambda e: e.dma_start(
            out=vbufB[0][:, :, :],
            in_=V["qV"].rearrange("(j p) c -> p j c", p=128)), inc=dPL, inc_n=16)
        b.op("pool", lambda e: e.dma_start(
            out=vbufB[1][:, :, :],
            in_=V["kV"].rearrange("(j p) c -> p j c", p=128)), inc=dPL, inc_n=16)
        kv_d = dPL.n
        b.wait("pe", dPL)
        b.wait("pe", cPOOL)
        b.wait("pe", cACT)
        ov_stage(vbufB[0], a_t, CCH, "qk", q_fm, range(TCH), 0, *OVB)
        q_ov_pe = cPE.n

        if stop <= 2:
            if dbg:
                b.wait("sp", cDVE)
                b.op("sp", lambda e: e.dma_start(
                    out=dbg_fm, in_=q_fm.rearrange("p j t -> p (j t)")),
                    inc=sOUT, inc_n=16)
                b.op("sp", lambda e: e.dma_start(
                    out=dbg_fm2, in_=a_t.rearrange("p j t -> p (j t)")),
                    inc=sOUT, inc_n=16)
            b.wait("sp", dSP)
            b.wait("sp", sOUT)
            return finish(nc, b)
        logits_stage(KT["kK"], h_fm, CCH, a_t, 4, 0, 1 / 32.0,
                     [0, 1, 2, 3], [4, 5, 6, 7])
        b.wait("pool", cPE, q_ov_pe)
        b.op("pool", lambda e: e.dma_start(
            out=vbufB[0][:, :, :],
            in_=V["vV"].rearrange("(j p) c -> p j c", p=128)), inc=dPL, inc_n=16)
        vv_d = dPL.n
        b.wait("pe", dPL, kv_d)
        b.wait("pe", cACT)
        ov_stage(vbufB[1], a_t, CCH, "qk", k_fm, range(TCH), 0, *OVB)

        if stop <= 3:
            if dbg:
                b.wait("sp", cDVE)
                b.op("sp", lambda e: e.dma_start(
                    out=dbg_fm, in_=k_fm.rearrange("p j t -> p (j t)")),
                    inc=sOUT, inc_n=16)
            b.wait("sp", dSP)
            b.wait("sp", sOUT)
            return finish(nc, b)
        logits_stage(KT["vK"], h_fm, CCH, a_t, 4, 0, 1 / 32.0,
                     [0, 1, 2, 3], [4, 5, 6, 7])
        b.wait("pool", cPE)
        b.op("pool", lambda e: e.memset(
            vext[:, :, 0:1040].rearrange("p i (h e) -> p i h e", e=65)[:, :, :, 64:65],
            1.0), inc=cPOOL)
        b.wait("pe", dPL, vv_d)
        b.wait("pe", cPOOL)
        b.wait("pe", cACT)
        ov_stage(vbufB[0], a_t, CCH, "vext", None, range(TCH), 0, *OVB)

        if stop <= 4:
            if dbg:
                b.wait("sp", cDVE)
                b.op("sp", lambda e: e.dma_start(
                    out=dbg_bf, in_=vext.rearrange("p i e -> p (i e)")),
                    inc=sOUT, inc_n=16)
            b.wait("sp", dSP)
            b.wait("sp", sOUT)
            return finish(nc, b)
        p_t2 = bfr(R2, 16 * 1024).rearrange("p (k t) -> p k t", t=1024)
        b.wait("pe", dSP, consts_loaded)
        b.wait("pe", cDVE)
        exp_par = {0: 0, 1: 0}
        o_evac = {}
        gidx = 0
        for hp in range(CCH):
            hA, hB = 2 * hp, 2 * hp + 1
            for qt in range(4):
                kmax = 4 * qt + 4
                mask_snap = cDVE.n
                for kt in range(kmax):
                    par = kt % 2
                    bA, bB = 2 * par, 2 * par + 1
                    b.wait("pe", cACT, exp_par[par])
                    b.op("pe", lambda e, hp=hp, kt=kt, qt=qt, bA=bA: e.matmul(
                        pb[bA][:],
                        k_fm[0:64, hp, kt * 128:(kt + 1) * 128],
                        q_fm[0:64, hp, qt * 512:(qt + 1) * 512],
                        start=True, stop=True, tile_position=(0, 0)))
                    b.op("pe", lambda e, hp=hp, kt=kt, qt=qt, bB=bB: e.matmul(
                        pb[bB][:],
                        k_fm[64:128, hp, kt * 128:(kt + 1) * 128],
                        q_fm[64:128, hp, qt * 512:(qt + 1) * 512],
                        start=True, stop=True, tile_position=(64, 0)), inc=cPE)
                    b.wait("act", cPE, cPE.n)
                    b.op("act", lambda e, kt=kt, par=par: e.activation(
                        out=p_t2[:, kt, :], in_=pq[par][:], func=AF.Exp,
                        scale=0.125), inc=cACT)
                    exp_par[par] = cACT.n
                    if kt >= 4 * qt:
                        r = kt - 4 * qt
                        off = 384 - 128 * r
                        b.wait("dve", cACT, cACT.n)
                        b.op("dve", lambda e, kt=kt, off=off: e.tensor_tensor(
                            p_t2[:, kt, 0:512], p_t2[:, kt, 0:512],
                            masks[:, off:off + 512], ALU.mult))
                        b.op("dve", lambda e, kt=kt, off=off: e.tensor_tensor(
                            p_t2[:, kt, 512:1024], p_t2[:, kt, 512:1024],
                            masks[:, off:off + 512], ALU.mult), inc=cDVE)
                        mask_snap = cDVE.n
                exp_all = max(exp_par.values())
                b.wait("pe", cACT, exp_all)
                b.wait("pe", cDVE, mask_snap)
                for h2 in range(2):
                    hh = hA if h2 == 0 else hB
                    for sub in range(4):
                        bank = 4 + gidx % 4
                        if gidx - 4 in o_evac:
                            b.wait("pe", cDVE, o_evac[gidx - 4])
                        for kt in range(kmax):
                            b.op("pe", lambda e, h2=h2, kt=kt, sub=sub, hh=hh,
                                 bank=bank, kmax=kmax: e.matmul(
                                     pb[bank][:, 0:65],
                                     p_t2[:, kt, h2 * 512 + sub * 128:
                                          h2 * 512 + (sub + 1) * 128],
                                     vext[:, kt, hh * 65:hh * 65 + 65],
                                     start=(kt == 0), stop=(kt == kmax - 1)),
                                 inc=cPE if kt == kmax - 1 else None)
                        b.wait("dve", cPE, cPE.n)
                        rs = 2 + gidx % 2
                        b.op("dve", lambda e, bank=bank, rs=rs: e.reciprocal(
                            rcp[:, rs:rs + 1], pb[bank][:, 64:65]), inc=cDVE)
                        b.wait("dve", cDVE)
                        b.op("dve", lambda e, bank=bank, qt=qt, sub=sub, hh=hh,
                             rs=rs: e.tensor_scalar(
                                 o_tm[:, qt * 4 + sub, hh * 64:(hh + 1) * 64],
                                 pb[bank][:, 0:64], rcp[:, rs:rs + 1], None,
                                 ALU.mult), inc=cDVE)
                        o_evac[gidx] = cDVE.n
                        gidx += 1

        if stop <= 5:
            if dbg:
                b.wait("sp", cDVE)
                b.op("sp", lambda e: e.dma_start(
                    out=dbg_bf[:, 0:TCH * 1024],
                    in_=o_tm.rearrange("p i c -> p (i c)")), inc=sOUT, inc_n=16)
            b.wait("sp", dSP)
            b.wait("sp", sOUT)
            return finish(nc, b)
        for tc in range(TCH):
            transpose_tm_to_fm_from_dve(o_tm[:, tc, :], o_fm, tc, bank=7)
        logits_stage(KT["pK"], o_fm, CCH, a_t, 4, 0, 1 / 32.0,
                     [0, 1, 2, 3], [4, 5, 6, 7])
        b.wait("pool", cPE)
        b.op("pool", lambda e: e.dma_start(
            out=vbufD[:, :, :],
            in_=V["pV"].rearrange("(j p) c -> p j c", p=128)), inc=dPL, inc_n=16)
        b.wait("pe", dPL)
        b.wait("pe", cACT)

        d_park = {}
        d_lnr = {}
        d_pend = {}
        d_trev = {}

        def d_tr_flush(tc):
            snap = d_pend.pop(tc)
            stg = sbf[tc % 2]
            b.wait("pe", cACT, snap)
            if tc - 1 in d_trev:
                b.wait("pe", cDVE, d_trev[tc - 1])
            pt = pbb[7]
            for j in range(CCH):
                b.op("pe", lambda e, j=j, pt=pt, stg=stg: e.transpose(
                    pt[:, j * 128:(j + 1) * 128],
                    stg[:, j * 128:(j + 1) * 128], idb[:]),
                    inc=cPE if j == CCH - 1 else None)
            b.wait("dve", cPE, cPE.n)
            src_v = pt[:, 0:1024].rearrange("p (j t) -> p j t", t=128)
            b.op("dve", lambda e, tc=tc, src_v=src_v: e.tensor_copy(
                lnafm[:, :, tc * 128:(tc + 1) * 128], src_v), inc=cDVE)
            d_trev[tc] = cDVE.n

        def d_extra(tc, sg, sg_act):
            slot = xt[2] if tc % 2 == 0 else xt[0]
            if tc >= 2:
                b.wait("sp", dSP, d_park[tc - 2])
                b.wait("sp", cACT, d_lnr[tc - 2])
            b.wait("sp", cPOOL)
            b.wait("sp", dSP, park_done[tc])
            b.op("sp", lambda e, tc=tc, s=slot: e.dma_start(
                out=s[:], in_=xn_park[tc * 128:(tc + 1) * 128, :]),
                inc=dSP, inc_n=16)
            ld = dSP.n
            b.wait("pool", dSP, ld)
            b.wait("pool", cACT, sg_act)
            b.op("pool", lambda e, s=slot, sg=sg: e.tensor_tensor(
                s[:], s[:], sg[:], ALU.add), inc=cPOOL)
            add_snap = cPOOL.n
            b.wait("sp", cPOOL, add_snap)
            b.op("sp", lambda e, tc=tc, s=slot: e.dma_start(
                out=a1_park[tc * 128:(tc + 1) * 128, :], in_=s[:]),
                inc=dSP, inc_n=16)
            d_park[tc] = dSP.n
            b.wait("dve", cPOOL, add_snap)
            ln_tile(slot[:], sbf[tc % 2][:], st=tc % 2)
            d_lnr[tc] = cACT.n
            d_pend[tc] = cACT.n
            if tc - 1 in d_pend:
                d_tr_flush(tc - 1)

        ov_stage(vbufD, a_t, CCH, "f32", None, range(TCH), 0, *OVB, extra=d_extra)
        for _tc in sorted(d_pend):
            d_tr_flush(_tc)

        if stop <= 6:
            if dbg:
                b.wait("sp", cDVE)
                b.op("sp", lambda e: e.dma_start(
                    out=dbg_fm, in_=lnafm.rearrange("p j t -> p (j t)")),
                    inc=sOUT, inc_n=16)
                b.op("sp", lambda e: e.dma_start(
                    out=dbg_fm2, in_=o_fm.rearrange("p j t -> p (j t)")),
                    inc=sOUT, inc_n=16)
                b.wait("sp", dSP, d_park[TCH - 1])
                b.op("sp", lambda e: e.dma_start(out=dbg_tm, in_=a1_park),
                    inc=sOUT, inc_n=16)
            b.wait("sp", dSP)
            b.wait("sp", sOUT)
            return finish(nc, b)
        d_pe_done = cPE.n
        e_park = {}
        fv_d = None
        for pss in range(2):
            t_off = pss * 1024
            b.wait("pe", cDVE)
            e1_mm, e1_ex = {}, {}
            for n in range(FCH):
                par = n % 2
                bks = (0, 1) if par == 0 else (2, 3)
                if n >= 3:
                    b.wait("pool", cPE, e1_mm[n - 3])
                else:
                    b.wait("pool", cPE)
                b.op("pool", lambda e, n=n: e.dma_start(
                    out=wlog[:, n % 4, :, :],
                    in_=fKT_h[:, n * 128:(n + 1) * 128].rearrange(
                        "(j p) n -> p j n", p=128)), inc=dPL, inc_n=16)
                if pss == 0 and n < 8:
                    b.op("pool", lambda e, n=n: e.dma_start(
                        out=fvbuf[:, 4 * n:4 * n + 4, :],
                        in_=fV_h[n * 512:(n + 1) * 512, :].rearrange(
                            "(j p) c -> p j c", p=128)), inc=dPL, inc_n=16)
                    if n == 7:
                        fv_d, fv_c = dPL.n, cPOOL.n
                b.wait("pe", dPL)
                if n >= 2:
                    b.wait("pe", cACT, e1_ex[n - 2])
                for c in range(CCH):
                    for tt in range(2):
                        b.op("pe", lambda e, n=n, c=c, tt=tt, bks=bks, t_off=t_off:
                             e.matmul(
                                 pb[bks[tt]][:], wlog[:, n % 4, c, :],
                                 lnafm[:, c, t_off + tt * 512:t_off + (tt + 1) * 512],
                                 start=(c == 0), stop=(c == CCH - 1)),
                             inc=cPE if (c == CCH - 1 and tt == 1) else None)
                e1_mm[n] = cPE.n
                b.wait("act", cPE, e1_mm[n])
                b.op("act", lambda e, n=n, bks=bks: e.activation(
                    out=a_tf[:, n, 0:1024],
                    in_=pq[bks[0] // 2][:], func=AF.Exp, scale=1 / 32.0), inc=cACT)
                e1_ex[n] = cACT.n
            b.wait("pe", dPL, fv_d)
            b.wait("pe", cPOOL, fv_c)
            b.wait("pe", cACT)

            def e_extra(tc, sg, sg_act):
                slot = xt[2]
                if tc - 1 in e_park:
                    b.wait("sp", dSP, e_park[tc - 1])
                b.wait("sp", cPOOL)
                b.wait("sp", dSP, d_park[tc])
                b.op("sp", lambda e, tc=tc, s=slot: e.dma_start(
                    out=s[:], in_=a1_park[tc * 128:(tc + 1) * 128, :]),
                    inc=dSP, inc_n=16)
                ld = dSP.n
                b.wait("pool", dSP, ld)
                b.wait("pool", cACT, sg_act)
                b.op("pool", lambda e, s=slot, sg=sg: e.tensor_tensor(
                    s[:], s[:], sg[:], ALU.add), inc=cPOOL)
                b.wait("sp", cPOOL)
                b.op("sp", lambda e, tc=tc, s=slot: e.dma_start(
                    out=acc_park[tc * 128:(tc + 1) * 128, :], in_=s[:]),
                    inc=dSP, inc_n=16)
                e_park[tc] = dSP.n

            ov_stage(fvbuf, a_tf, FCH, "f32", None,
                     range(pss * 8, pss * 8 + 8), pss * 8, *OVB, extra=e_extra)
            e1_last = e1_mm

        if stop <= 7:
            if dbg:
                b.wait("sp", dSP, e_park[TCH - 1])
                b.op("sp", lambda e: e.dma_start(out=dbg_tm, in_=acc_park),
                    inc=sOUT, inc_n=16)
            b.wait("sp", dSP)
            b.wait("sp", sOUT)
            return finish(nc, b)
        out_done = {}
        f5_add = {}
        wbig_gate = None
        prev_f1 = None
        prev_f4ev = None
        f5_dve = 0
        for pss in range(2):
            t_off = pss * 1024
            if pss == 0:
                b.wait("sp", cPE, e1_last[FCH - 1])
            else:
                b.wait("sp", cPE, prev_f1[FCH - 1])
            b.wait("sp", dSP, xnfm_parked)
            b.op("sp", lambda e, t_off=t_off: e.dma_start(
                out=xn_ffn[:, :, :],
                in_=xnfm_park.rearrange("p (j t) -> p j t", t=T)[
                    :, :, t_off:t_off + 1024]),
                inc=dSP, inc_n=16)
            xn_ldd = dSP.n

            f1_mm, f1_ev = {}, {}
            for o in range(FCH):
                par = o % 2
                bks = (0, 1, 2, 3) if par == 0 else (4, 5, 6, 7)
                if o >= 3:
                    b.wait("pool", cPE, f1_mm[o - 3])
                elif pss == 0:
                    b.wait("pool", cPE, e1_last[FCH - 3 + o])
                else:
                    b.wait("pool", cPE, prev_f1[FCH - 3 + o])
                b.op("pool", lambda e, o=o: e.dma_start(
                    out=wlog[:, o % 4, :, :],
                    in_=w1T_h[:, o * 128:(o + 1) * 128].rearrange(
                        "(j p) n -> p j n", p=128)), inc=dPL, inc_n=16)
                if o >= 2:
                    b.wait("pool", cPE, f1_mm[o - 2])
                elif wbig_gate is not None:
                    b.wait("pool", cPE, wbig_gate[6 + o])
                b.op("pool", lambda e, o=o: e.dma_start(
                    out=wbig[:, o % 2, 0:CCH, :],
                    in_=w2T_h[:, o * 128:(o + 1) * 128].rearrange(
                        "(j p) n -> p j n", p=128)), inc=dPL, inc_n=16)
                b.wait("pe", dPL)
                b.wait("pe", dSP, xn_ldd)
                if o >= 2:
                    b.wait("pe", cDVE, f1_ev[o - 2])
                elif pss == 1:
                    if o == 0:
                        b.wait("pe", cACT, prev_f4ev)
                    else:
                        b.wait("pe", cDVE, f5_dve)
                else:
                    b.wait("pe", cDVE)
                for c in range(CCH):
                    for tt in range(2):
                        b.op("pe", lambda e, o=o, c=c, tt=tt, bks=bks: e.matmul(
                            pb[bks[tt]][:], wlog[:, o % 4, c, :],
                            xn_ffn[:, c, tt * 512:(tt + 1) * 512],
                            start=(c == 0), stop=(c == CCH - 1)))
                for c in range(CCH):
                    for tt in range(2):
                        b.op("pe", lambda e, o=o, c=c, tt=tt, bks=bks: e.matmul(
                            pb[bks[2 + tt]][:], wbig[:, o % 2, c, :],
                            xn_ffn[:, c, tt * 512:(tt + 1) * 512],
                            start=(c == 0), stop=(c == CCH - 1)),
                            inc=cPE if (c == CCH - 1 and tt == 1) else None)
                f1_mm[o] = cPE.n
                b.wait("act", cPE, f1_mm[o])
                b.wait("act", cDVE)
                b.op("act", lambda e, o=o, bks=bks: e.activation(
                    out=u_buf[:, o, 0:1024], in_=pq[bks[0] // 2][:], func=AF.Silu,
                    scale=sc["s1"][:, o:o + 1]), inc=cACT)
                b.op("act", lambda e, o=o, bks=bks: e.activation(
                    out=sbf[0][:, 0:1024], in_=pq[bks[2] // 2][:], func=AF.Identity,
                    scale=sc["s2"][:, o:o + 1]), inc=cACT)
                b.wait("dve", cACT)
                b.op("dve", lambda e, o=o: e.tensor_tensor(
                    u_buf[:, o, :], u_buf[:, o, :], sbf[0][:], ALU.mult), inc=cDVE)
                f1_ev[o] = cDVE.n

            wbig_gate2 = f1_mm
            for w_h, s_nm, src, dst in ((h1T_h, "sh1", u_buf, g_buf),
                                        (h2T_h, "sh2", g_buf, u_buf)):
                f2_mm, f2_ev = {}, {}
                for o in range(FCH):
                    par = o % 2
                    bk = (0, 1) if par == 0 else (2, 3)
                    if o >= 2:
                        b.wait("pool", cPE, f2_mm[o - 2])
                    else:
                        b.wait("pool", cPE, wbig_gate2[FCH - 2 + o]
                               if len(wbig_gate2) >= FCH else wbig_gate2[6 + o])
                    b.op("pool", lambda e, o=o, w_h=w_h: e.dma_start(
                        out=wbig[:, o % 2, :, :],
                        in_=w_h[:, o * 128:(o + 1) * 128].rearrange(
                            "(j p) n -> p j n", p=128)), inc=dPL, inc_n=16)
                    b.wait("pe", dPL)
                    if o >= 2:
                        b.wait("pe", cACT, f2_ev[o - 2])
                    for c in range(FCH):
                        for tt in range(2):
                            b.op("pe", lambda e, o=o, c=c, tt=tt, bk=bk, src=src:
                                 e.matmul(
                                     pb[bk[tt]][:], wbig[:, o % 2, c, :],
                                     src[:, c, tt * 512:(tt + 1) * 512],
                                     start=(c == 0), stop=(c == FCH - 1)),
                                 inc=cPE if (c == FCH - 1 and tt == 1) else None)
                    f2_mm[o] = cPE.n
                    b.wait("act", cPE, f2_mm[o])
                    b.op("act", lambda e, o=o, bk=bk, dst=dst, s_nm=s_nm:
                         e.activation(
                             out=dst[:, o, 0:1024], in_=pq[bk[0] // 2][:],
                             func=AF.Gelu, scale=sc[s_nm][:, o:o + 1]), inc=cACT)
                    f2_ev[o] = cACT.n
                wbig_gate2 = f2_mm

            f4_mm, f4_ev = {}, {}
            for cc in range(CCH):
                par = cc % 2
                bk = (0, 1) if par == 0 else (2, 3)
                if cc >= 2:
                    b.wait("pool", cPE, f4_mm[cc - 2])
                else:
                    b.wait("pool", cPE, wbig_gate2[FCH - 2 + cc])
                b.op("pool", lambda e, cc=cc: e.dma_start(
                    out=wbig[:, cc % 2, :, :],
                    in_=w3T_h[:, cc * 128:(cc + 1) * 128].rearrange(
                        "(j p) n -> p j n", p=128)), inc=dPL, inc_n=16)
                b.wait("pe", dPL)
                if cc >= 2:
                    b.wait("pe", cACT, f4_ev[cc - 2])
                for c in range(FCH):
                    for tt in range(2):
                        b.op("pe", lambda e, cc=cc, c=c, tt=tt, bk=bk: e.matmul(
                            pb[bk[tt]][:], wbig[:, cc % 2, c, :],
                            u_buf[:, c, tt * 512:(tt + 1) * 512],
                            start=(c == 0), stop=(c == FCH - 1)),
                            inc=cPE if (c == FCH - 1 and tt == 1) else None)
                f4_mm[cc] = cPE.n
                b.wait("act", cPE, f4_mm[cc])
                b.op("act", lambda e, cc=cc, bk=bk: e.activation(
                    out=y_buf[:, cc, 0:1024], in_=pq[bk[0] // 2][:],
                    func=AF.Identity, scale=sc["s3"][:, cc:cc + 1]), inc=cACT)
                f4_ev[cc] = cACT.n
            y_done = cACT.n
            wbig_gate = {6 + i: f4_mm[6 + i] for i in range(2)}
            prev_f1 = f1_mm
            prev_f4ev = f4_ev[CCH - 1]

            for tci in range(8):
                tc = pss * 8 + tci
                sx, sa = xt[tci % 2], xt[2]
                if tc - 1 in out_done:
                    b.wait("sp", sOUT, out_done[tc - 1])
                b.wait("sp", cDVE)
                b.wait("sp", cPOOL)
                b.wait("sp", dSP, e_park[tc])
                b.op("sp", lambda e, tc=tc, sx=sx: e.dma_start(
                    out=sx[:], in_=x_h[tc * 128:(tc + 1) * 128, :]),
                    inc=dSP, inc_n=16)
                b.op("sp", lambda e, tc=tc, sa=sa: e.dma_start(
                    out=sa[:], in_=acc_park[tc * 128:(tc + 1) * 128, :]),
                    inc=dSP, inc_n=16)
                ldd = dSP.n
                b.wait("pe", cACT, y_done)
                fb = (6, 7)
                if tc - 1 in f5_add:
                    b.wait("pe", cDVE, f5_add[tc - 1])
                for j in range(CCH):
                    bank = fb[j // 4]
                    b.op("pe", lambda e, j=j, tci=tci, bank=bank: e.transpose(
                        pb[bank][:, (j % 4) * 128:(j % 4 + 1) * 128],
                        y_buf[:, j, tci * 128:(tci + 1) * 128], idf[:]),
                        inc=cPE if j == CCH - 1 else None)
                tr_snap = cPE.n
                b.wait("pool", dSP, ldd)
                b.op("pool", lambda e, sx=sx, sa=sa: e.tensor_tensor(
                    sx[:], sx[:], sa[:], ALU.add), inc=cPOOL)
                b.wait("dve", cPE, tr_snap)
                b.wait("dve", cPOOL)
                b.op("dve", lambda e, sx=sx, fb=fb: e.tensor_tensor(
                    sx[:, 0:512], sx[:, 0:512], pb[fb[0]][:], ALU.add))
                b.op("dve", lambda e, sx=sx, fb=fb: e.tensor_tensor(
                    sx[:, 512:1024], sx[:, 512:1024], pb[fb[1]][:], ALU.add),
                    inc=cDVE)
                f5_add[tc] = cDVE.n
                b.wait("sp", cDVE)
                b.op("sp", lambda e, tc=tc, sx=sx: e.dma_start(
                    out=out_h[tc * 128:(tc + 1) * 128, :], in_=sx[:]),
                    inc=sOUT, inc_n=16)
                out_done[tc] = sOUT.n
                f5_dve = cDVE.n

        b.wait("sp", dSP)
        b.wait("sp", sOUT)
        return finish(nc, b)


def finish(nc, b):
    with ExitStack() as _fs:
        with nc.Block() as block:
            @block.sync
            def _(e):
                for f in b.q["sp"]:
                    f(e)

            @block.gpsimd
            def _(e):
                for f in b.q["pool"]:
                    f(e)

            @block.tensor
            def _(e):
                for f in b.q["pe"]:
                    f(e)

            @block.scalar
            def _(e):
                for f in b.q["act"]:
                    f(e)

            @block.vector
            def _(e):
                for f in b.q["dve"]:
                    f(e)
    return nc


_NC_CACHE = {}


def kernel(**inputs):
    def f32(a):
        return np.ascontiguousarray(np.asarray(a, dtype=np.float32))

    x = f32(inputs["L0_attn_x"])
    shared = {
        "qKT": f32(np.asarray(inputs["L0_attn_qK"]).T),
        "kKT": f32(np.asarray(inputs["L0_attn_kK"]).T),
        "vKT": f32(np.asarray(inputs["L0_attn_vK"]).T),
        "pKT": f32(np.asarray(inputs["L0_attn_pK"]).T),
        "qV": f32(inputs["L0_attn_qV"]),
        "kV": f32(inputs["L0_attn_kV"]),
        "vV": f32(inputs["L0_attn_vV"]),
        "pV": f32(inputs["L0_attn_pV"]),
        "fKT": f32(np.asarray(inputs["L0_attn_fK"]).T),
        "fV": f32(inputs["L0_attn_fV"]),
        "w1T": f32(np.asarray(inputs["L0_ffn_w1_w"]).T),
        "w2T": f32(np.asarray(inputs["L0_ffn_w2_w"]).T),
        "h1T": f32(np.asarray(inputs["L0_ffn_h1_w"]).T),
        "h2T": f32(np.asarray(inputs["L0_ffn_h2_w"]).T),
        "w3T": f32(np.asarray(inputs["L0_ffn_w3_w"]).T),
        "s1": f32(np.asarray(inputs["L0_ffn_w1_s"]).reshape(FCH, 128).T),
        "s2": f32(np.asarray(inputs["L0_ffn_w2_s"]).reshape(FCH, 128).T),
        "sh1": f32(np.asarray(inputs["L0_ffn_h1_s"]).reshape(FCH, 128).T),
        "sh2": f32(np.asarray(inputs["L0_ffn_h2_s"]).reshape(FCH, 128).T),
        "s3": f32(np.asarray(inputs["L0_ffn_w3_s"]).reshape(CCH, 128).T),
        "idb": np.eye(128, dtype=ml_dtypes.bfloat16),
        "idf": np.eye(128, dtype=np.float32),
    }
    m = np.zeros((4, 128, 512), dtype=ml_dtypes.bfloat16)
    i = np.arange(128)[:, None]
    j = np.arange(512)[None, :]
    for r in range(4):
        m[r] = (j >= i + r * 128).astype(ml_dtypes.bfloat16)
    shared["mask"] = m

    if "nc" not in _NC_CACHE:
        _NC_CACHE["nc"] = build_nc()
    nc = _NC_CACHE["nc"]
    in_maps = [dict(shared, x=np.ascontiguousarray(x[b])) for b in range(8)]
    res = run_bass_kernel_spmd(nc, in_maps, core_ids=list(range(8)))
    return np.stack([np.asarray(res.results[i]["out"], dtype=np.float32)
                     for i in range(8)], axis=0)
